# version 01
# baseline (speedup 1.0000x reference)
import contextlib
import numpy as np
import ml_dtypes
import concourse.bass as bass
import concourse.mybir as mybir
from concourse.bass_utils import run_bass_kernel_spmd

F32 = mybir.dt.float32
BF16 = mybir.dt.bfloat16
AF = mybir.ActivationFunctionType
ALU = mybir.AluOpType
AX = mybir.AxisListType

HID = 4096
NH, PH, NS, NG, KCV = 128, 64, 128, 8, 4
INTER = NH * PH
CONVD = INTER + 2 * NG * NS
PROJ = INTER + CONVD + NH
CHUNK = 128
EPS = 1e-5
NCORES = 8
NPREF = 2
PAD = NPREF * CHUNK + 3
NE_GATE = INTER // 128
NE_HBC = CONVD // 128
NE = PROJ // 128
KD = HID // 128
R_LORA, NEXP = 16, 8
ER = NEXP * R_LORA
NEG = -3.0e38


def _tl(x):
    if x is None:
        return []
    if isinstance(x, list):
        out = []
        for e in x:
            out.extend(_tl(e))
        return out
    return [x]


class Prog:
    ENGS = ("tensor", "vector", "scalar", "gpsimd", "sync")

    def __init__(self, nc, ctx):
        self.nc = nc
        self.ctx = ctx
        self.q = {e: [] for e in self.ENGS}
        self.sems = {}
        self.val = {}
        self.waited = {e: {} for e in self.ENGS}
        for name in ("pe", "ve", "ae", "ge"):
            self.sems[name] = ctx.enter_context(nc.semaphore(name))
            self.val[name] = 0

    AUTO = {"vector": "ve", "scalar": "ae", "gpsimd": "ge"}

    def emit(self, eng, fn, waits=(), sig=None, n=1):
        waits = list(waits)
        if eng in self.AUTO:
            a = self.AUTO[eng]
            if self.val[a] > 0:
                waits.append((a, self.val[a]))
            if sig is None:
                sig = a
        wl0 = _tl(list(waits))
        tok = None
        wl = []
        for s, v in wl0:
            if s.startswith("D_"):
                v = self.val[s]
            if self.waited[eng].get(s, 0) >= v:
                continue
            self.waited[eng][s] = v
            wl.append((s, v))
        if sig is not None:
            self.val[sig] += n
            tok = (sig, self.val[sig])
        sems = self.sems

        def closure(e, wl=wl, fn=fn, tok=tok, n=n):
            for s, v in wl:
                e.wait_ge(sems[s], v)
            inst = fn(e)
            if tok is not None:
                inst.then_inc(sems[tok[0]], n)
        self.q[eng].append(closure)
        return tok

    def fence(self, *streams):
        return [("D_" + st, self.val["D_" + st]) for st in streams
                if ("D_" + st) in self.val]

    def dma(self, queue, stream, out, in_, waits=(), **kw):
        sig = "D_" + stream
        if sig not in self.sems:
            self.sems[sig] = self.ctx.enter_context(self.nc.semaphore(sig))
            self.val[sig] = 0
        return self.emit(queue, lambda e: e.dma_start(out=out, in_=in_, **kw),
                         waits=waits, sig=sig, n=16)


class Arena:

    def __init__(self, t):
        self.t = t
        self.n = t.shape[1]
        self.off = 0
        self.marks = []

    def mark(self):
        self.marks.append(self.off)

    def release(self):
        self.off = self.marks.pop()

    def f32(self, cols):
        cols = (cols + 1) // 2 * 2
        assert self.off + cols <= self.n, f"arena overflow {self.off}+{cols}>{self.n}"
        ap = self.t[:, self.off:self.off + cols]
        self.off += cols
        return ap

    def bf16(self, cols):
        return self.f32((cols + 1) // 2).bitcast(BF16)[:, 0:cols]


def bfree(ap, reps, inner):
    return ap.unsqueeze(2).broadcast_to([ap.shape[0], reps, inner])


def bmid(ap, mid):
    return ap.unsqueeze(1).broadcast_to([ap.shape[0], mid, ap.shape[1]])


def r3(ap, inner):
    return ap.rearrange("p (a b) -> p a b", b=inner)


def _finish(P, block, nc):
    @block.tensor
    def _(eng):
        for f in P.q["tensor"]:
            f(eng)

    @block.vector
    def _(eng):
        for f in P.q["vector"]:
            f(eng)

    @block.scalar
    def _(eng):
        for f in P.q["scalar"]:
            f(eng)

    @block.gpsimd
    def _(eng):
        for f in P.q["gpsimd"]:
            f(eng)

    @block.sync
    def _(eng):
        for f in P.q["sync"]:
            f(eng)
    return nc


def build_program(n_main=16, upto='p4'):
    nc = bass.Bass("TRN2", target_bir_lowering=False, debug=False)
    NZ = NPREF + n_main
    if upto == "p2pre":
        NZ_RUN = NPREF
    elif upto.startswith("p2v") or upto == "p2one":
        NZ_RUN = NPREF + 1
    else:
        NZ_RUN = NZ
    FLVL = int(upto[3]) if upto.startswith("p2v") else 9
    NT = NZ * CHUNK
    NM = n_main * CHUNK
    NR = NT + 3
    NTT = n_main
    NCOL = (NM + 511) // 512

    def ein(name, shape, dt=F32):
        return nc.dram_tensor(name, list(shape), dt, kind="ExternalInput").ap()

    xin = ein("xin", [NR, HID])
    wint = ein("wint", [NE, 128, KD, 128])
    wot = ein("wot", [32, 128, 64, 128])
    wbt = ein("wbt", [32, 128, 32, 128])
    convw = ein("convw", [128, NE_HBC * KCV])
    convb = ein("convb", [128, NE_HBC])
    nwc = ein("nwc", [128, INTER // 128])
    bbc = ein("bbc", [128, 32])
    dtbc = ein("dtbc", [128, 1])
    alogc = ein("alogc", [128, 1])
    drow = ein("drow", [1, 128])
    wrt = ein("wrt", [128, KD, 8])
    alt = ein("alt", [128, KD, ER])
    blt = ein("blt", [ER, HID])
    identf = ein("identf", [128, 128])
    identb = ein("identb", [128, 128], BF16)
    trimask = ein("trimask", [128, 128])
    onesrow = ein("onesrow", [1, 128])
    smask = ein("smask", [128, 1])

    out = nc.dram_tensor("out", [NM, HID], F32, kind="ExternalOutput").ap()

    def scrt(name, shape, dt):
        return nc.dram_tensor(name, list(shape), dt).ap()

    hsT_s = scrt("hsT_s", [NT, INTER], BF16)
    sg_s = scrt("sg_s", [NM, INTER], BF16)
    b_s = scrt("b_s", [NG * NS, NT], BF16)
    c_s = scrt("c_s", [NG * NS, NT], BF16)
    acum_s = scrt("acum_s", [128, NT], F32)
    y_s = scrt("y_s", [NM, INTER], BF16)
    g_s = scrt("g_s", [INTER, NM], BF16)
    x1t_s = scrt("x1t_s", [HID, NM], BF16)
    x1p_s = scrt("x1p_s", [NM, HID], F32)
    ffp_s = scrt("ffp_s", [NM, HID], F32)

    ctx = contextlib.ExitStack()
    with ctx:
        P = Prog(nc, ctx)

        def sb(name, shape, dt):
            return ctx.enter_context(nc.sbuf_tensor(name, list(shape), dt))

        idf = sb("idf", [128, 128], F32)
        idb = sb("idb", [128, 128], BF16)
        trim = sb("trim", [128, 128], F32)
        ones_f = sb("ones_f", [1, 128], F32)
        cwc = sb("cwc", [128, NE_HBC * KCV], F32)
        cbc = sb("cbc", [128, NE_HBC], F32)
        nwcs = sb("nwcs", [128, INTER // 128], F32)
        bbcs = sb("bbcs", [128, 32], F32)
        dtbs = sb("dtbs", [128, 1], F32)
        alogs = sb("alogs", [128, 1], F32)
        acol = sb("acol", [128, 1], F32)
        drows = sb("drows", [1, 128], F32)
        drep = sb("drep", [128, 128], BF16)
        smk = sb("smk", [128, 1], F32)
        wrts = sb("wrts", [128, KD * 8], BF16)
        dtT_all = sb("dtT_all", [128, NT], BF16)
        dsT_all = sb("dsT_all", [128, NT], BF16)
        nacT_all = sb("nacT_all", [128, NT], F32)
        arow_all = sb("arow_all", [1, NT], F32)
        mu1 = sb("mu1", [128, max(NTT, 1)], F32)
        nmr1 = sb("nmr1", [128, max(NTT, 1)], F32)
        rs1 = sb("rs1", [128, max(NTT, 1)], F32)
        ssq = sb("ssq", [128, 8], F32)
        exc = sb("exc", [128, 1], F32)
        vtmp = sb("vtmp", [128, 1], F32)
        rstd8 = sb("rstd8", [128, 8], F32)
        rstdb = sb("rstdb", [128, 8], BF16)
        combt = sb("combt", [128, 8], F32)
        zrow = sb("zrow", [128, 8], F32)
        qrow = sb("qrow", [128, 8], F32)
        m1c = sb("m1c", [128, 1], F32)
        m2c = sb("m2c", [128, 1], F32)
        denc = sb("denc", [128, 1], F32)
        gsb = [sb(f"gsb{i}", [128, 128], BF16) for i in range(2)]
        btk = [sb(f"btk{i}", [128, 128], BF16) for i in range(2)]
        amat = sb("amat", [128, 128], F32)
        afb = [sb(f"afb{i}", [1, 1024], F32) for i in range(2)]
        tstg = [sb(f"tstg{i}", [128, 128], BF16) for i in range(2)]
        gstg = [sb(f"gstg{i}", [128, 128], BF16) for i in range(2)]
        xblk = [sb(f"xblk{i}", [128, 128], F32) for i in range(2)]
        x1stg = [sb(f"x1stg{i}", [128, 128], F32) for i in range(2)]
        hwstg = [sb(f"hwstg{i}", [128, 128], BF16) for i in range(2)]
        ffstg = [sb(f"ffstg{i}", [128, 128], F32) for i in range(2)]
        ystg = [sb(f"ystg{i}", [128, 512], BF16) for i in range(2)]
        gcol = [sb(f"gcol{i}", [128, 512], BF16) for i in range(2)]

        AR = Arena(sb("AR", [128, 40256], F32))

        ps = ctx.enter_context(nc.psum_tensor("ps", [128, 4096], F32))
        psb = ps[:, :].bitcast(BF16)

        def bank(i, w=512, off=0):
            return ps[:, i * 512 + off: i * 512 + off + w]

        def bwin(i, w=128, off=0):
            return psb[:, i * 1024 + off: i * 1024 + off + w]

        block = ctx.enter_context(nc.Block())

        def MM(o, a, b, s1, s2, waits=(), sig="pe"):
            return P.emit("tensor",
                          lambda e, o=o, a=a, b=b, s1=s1, s2=s2:
                          e.matmul(o, a, b, start=s1, stop=s2),
                          waits=waits, sig=sig)

        def TR(o, i_, idn, waits=(), sig="pe"):
            return P.emit("tensor",
                          lambda e, o=o, i_=i_, idn=idn: e.transpose(o, i_, idn),
                          waits=waits, sig=sig)

        def VCOPY(o, i_, waits=(), sig="ve", eng="vector"):
            sg = {"vector": "ve", "gpsimd": "ge"}[eng]
            return P.emit(eng, lambda e, o=o, i_=i_: e.tensor_copy(o, i_),
                          waits=waits, sig=sg if sig else None)

        def VTT(o, a, b, op, waits=(), sig="ve", eng="vector"):
            sg = {"vector": "ve", "gpsimd": "ge"}[eng] if sig else None
            return P.emit(eng, lambda e, o=o, a=a, b=b, op=op:
                          e.tensor_tensor(o, a, b, op), waits=waits, sig=sg)

        def VTS(o, a, s1_, s2_, op0, op1=None, waits=(), sig="ve", accum=None):
            if op1 is None:
                return P.emit("vector",
                              lambda e, o=o, a=a, s1_=s1_, op0=op0:
                              e.tensor_scalar(o, a, s1_, None, op0),
                              waits=waits, sig=sig)
            return P.emit("vector",
                          lambda e, o=o, a=a, s1_=s1_, s2_=s2_, op0=op0, op1=op1,
                          accum=accum:
                          e.tensor_scalar(o, a, s1_, s2_, op0, op1,
                                          accum_out=accum),
                          waits=waits, sig=sig)

        def ACT(o, i_, fn, bias=0.0, scale=1.0, waits=(), sig="ae", accum=None):
            return P.emit("scalar",
                          lambda e, o=o, i_=i_, fn=fn, bias=bias, scale=scale,
                          accum=accum:
                          e.activation(o, i_, fn, bias=bias, scale=scale,
                                       accum_out=accum),
                          waits=waits, sig=sig)

        t_loads = []
        for dst, src in ((idf[:, :], identf), (idb[:, :], identb),
                         (trim[:, :], trimask), (ones_f[:, :], onesrow),
                         (cbc[:, :], convb), (cwc[:, :], convw),
                         (nwcs[:, :], nwc), (bbcs[:, :], bbc),
                         (dtbs[:, :], dtbc), (alogs[:, :], alogc),
                         (drows[:, :], drow), (smk[:, :], smask)):
            t_loads.append(P.dma("sync", "ini", dst, src))
        t_loads.append(P.dma("gpsimd", "iniw",
                             wrts[:, :].rearrange("p (k e) -> p k e", e=8), wrt))
        t0 = ACT(acol[:, 0:1], alogs[:, 0:1], AF.Exp, waits=t_loads)
        tk_acol = VTS(acol[:, 0:1], acol[:, 0:1], -1.0, None, ALU.mult, waits=[t0])
        t0 = MM(bank(7, 128), ones_f[0:1, :], drows[0:1, :], True, True,
                waits=t_loads)
        tk_drep = VCOPY(drep[:, :], bank(7, 128), waits=[t0])

        AR.mark()
        XTW = 1284
        xt = r3(AR.bf16(KD * XTW), XTW)
        xtile = [AR.bf16(HID) for _ in range(2)]
        wst = [r3(AR.bf16(KD * 128), 128) for _ in range(2)]
        cscr = [AR.f32(1284) for _ in range(2)]
        xsil = [AR.bf16(1284) for _ in range(2)]
        spl = AR.f32(1284)
        adt = AR.f32(1284)
        dtv = AR.f32(NT)
        acum = AR.f32(NT)

        if NR <= 1283:
            passes = [(0, NR, 0)]
        else:
            passes = [(0, 1283, 0), (1280, NR - 1280, 1280)]
        HBC0, HBC1 = NE_GATE, NE_GATE + NE_HBC

        last_wst = [None, None]
        last_epil = [None, None]
        last_xsil_u = [None, None]
        last_cscr_u = [None, None]
        last_tstg_u = [None, None]
        last_xtile_u = [None, None]
        tk_b7 = tk_drep
        sg_toks, hs_toks, bc_toks, dt_prep_toks = [], [], [], []
        mm_tok = None
        last_spl_u = None

        for (r0, W, q0) in passes:
            ntile = (W + 127) // 128
            ev_last = None
            for it in range(ntile):
                rlo = r0 + it * 128
                rows = min(128, r0 + W - rlo)
                xb = it % 2
                t_x = P.dma("gpsimd", f"xld{xb}", xtile[xb][0:rows, :], xin[rlo:rlo + rows, :],
                            waits=[last_xtile_u[xb], mm_tok])
                for k in range(KD):
                    slot = k % 2
                    o_ps = bwin(6 + slot, 128)
                    t_t = TR(o_ps[0:128, 0:rows],
                             xtile[xb][0:rows, k * 128:(k + 1) * 128],
                             idb[0:rows, 0:rows],
                             waits=[t_x, last_tstg_u[slot], tk_b7])
                    ev_last = VCOPY(xt[:, k, it * 128: it * 128 + rows],
                                    o_ps[0:128, 0:rows], waits=[t_t])
                    last_tstg_u[slot] = ev_last
                last_xtile_u[xb] = ev_last
            xt_ready = ev_last
            tk_b7 = ev_last

            cols = []
            off = 0
            while off < W:
                cw = min(512, W - off)
                cols.append((off, cw))
                off += cw

            for ei in range(NE):
                is_gate = ei < HBC0
                is_hbc = HBC0 <= ei < HBC1
                is_dt = ei == NE - 1
                glo = PAD - r0 if r0 < PAD else 3
                if is_gate and W - glo <= 0:
                    continue
                buf = ei % 2
                pb = 0 if buf == 0 else 3
                t_w = P.dma("gpsimd", f"wst{buf}", wst[buf][:, :, :], wint[ei, :, :, :],
                            waits=[last_wst[buf]])
                mm_tok = None
                for ci, (coff, cw) in enumerate(cols):
                    for k in range(KD):
                        mm_tok = MM(bank(pb + ci, cw), wst[buf][:, k, :],
                                    xt[:, k, coff:coff + cw],
                                    (k == 0), (k == KD - 1),
                                    waits=[t_w, xt_ready,
                                           last_epil[buf] if (k == 0 and ci == 0)
                                           else None],
                                    sig="pe" if k == KD - 1 else None)
                last_wst[buf] = mm_tok

                def pap(lo, w, pb=pb):
                    return ps[:, pb * 512 + lo: pb * 512 + lo + w]

                if is_gate:
                    gw = W - glo
                    m0 = (r0 + glo) - PAD
                    t_s = ACT(xsil[buf][:, 0:gw], pap(glo, gw), AF.Silu,
                              waits=[mm_tok, last_xsil_u[buf]])
                    last_epil[buf] = t_s
                    dtok = None
                    for sti in range(gw // 128):
                        slot = sti % 2
                        o_ps = bwin(6 + slot, 128)
                        t_t = TR(o_ps, xsil[buf][:, sti * 128:(sti + 1) * 128],
                                 idb[:, :], waits=[t_s, last_tstg_u[slot], tk_b7])
                        ev = VCOPY(tstg[slot][:, :], o_ps, waits=[t_t])
                        dtok = P.dma("sync", f"tt{slot}",
                                     sg_s[m0 + sti * 128: m0 + sti * 128 + 128,
                                          ei * 128:(ei + 1) * 128],
                                     tstg[slot][:, :], waits=[ev])
                        last_tstg_u[slot] = dtok
                        tk_b7 = ev
                    sg_toks.append(dtok)
                    last_xsil_u[buf] = dtok
                elif is_hbc:
                    hi = ei - HBC0
                    nq_t = W - 3
                    cp = VCOPY(cscr[buf][:, 0:W], pap(0, W),
                               waits=[mm_tok, last_cscr_u[buf]])
                    last_epil[buf] = cp
                    mac = P.emit("vector",
                                 lambda e, buf=buf, nq_t=nq_t, hi=hi:
                                 e.tensor_scalar(spl[:, 0:nq_t],
                                                 cscr[buf][:, 0:nq_t],
                                                 cwc[:, hi * 4: hi * 4 + 1], None,
                                                 ALU.mult),
                                 waits=[cp, last_spl_u])
                    for kk in (1, 2, 3):
                        mac = P.emit("vector",
                                     lambda e, buf=buf, nq_t=nq_t, hi=hi, kk=kk:
                                     e.scalar_tensor_tensor(
                                         spl[:, 0:nq_t],
                                         cscr[buf][:, kk:kk + nq_t],
                                         cwc[:, hi * 4 + kk: hi * 4 + kk + 1],
                                         spl[:, 0:nq_t], ALU.mult, ALU.add),
                                     sig="ve" if kk == 3 else None)
                    last_cscr_u[buf] = mac
                    t_s = ACT(xsil[buf][:, 0:nq_t], spl[:, 0:nq_t], AF.Silu,
                              bias=cbc[:, hi:hi + 1],
                              waits=[mac, last_xsil_u[buf]])
                    last_spl_u = t_s
                    if hi < 64:
                        dtok = None
                        for sti in range(nq_t // 128):
                            slot = sti % 2
                            o_ps = bwin(6 + slot, 128)
                            t_t = TR(o_ps,
                                     xsil[buf][:, sti * 128:(sti + 1) * 128],
                                     idb[:, :],
                                     waits=[t_s, last_tstg_u[slot], tk_b7])
                            ev = VCOPY(tstg[slot][:, :], o_ps, waits=[t_t])
                            dtok = P.dma("sync", f"tt{slot}",
                                         hsT_s[q0 + sti * 128:
                                               q0 + sti * 128 + 128,
                                               hi * 128:(hi + 1) * 128],
                                         tstg[slot][:, :], waits=[ev])
                            last_tstg_u[slot] = dtok
                            tk_b7 = ev
                        hs_toks.append(dtok)
                        last_xsil_u[buf] = dtok
                    else:
                        tgt = b_s if hi < 72 else c_s
                        row0 = (hi - 64) * 128 if hi < 72 else (hi - 72) * 128
                        dtok = P.dma("sync", f"bcw{buf}", tgt[row0:row0 + 128, q0:q0 + nq_t],
                                     xsil[buf][:, 0:nq_t], waits=[t_s])
                        bc_toks.append(dtok)
                        last_xsil_u[buf] = dtok
                elif is_dt:
                    nq_t = W - 3
                    t1 = VTS(spl[:, 0:nq_t], pap(3, nq_t), dtbs[:, 0:1], None,
                             ALU.add, waits=[mm_tok, last_cscr_u[buf],
                                             last_spl_u])
                    last_epil[buf] = t1
                    ACT(adt[:, 0:nq_t], spl[:, 0:nq_t], AF.Abs, waits=[t1],
                        sig=None)
                    ACT(adt[:, 0:nq_t], adt[:, 0:nq_t], AF.Exp, scale=-1.0,
                        sig=None)
                    t2 = ACT(adt[:, 0:nq_t], adt[:, 0:nq_t], AF.Ln, bias=1.0)
                    P.emit("vector", lambda e, nq_t=nq_t:
                           e.tensor_relu(spl[:, 0:nq_t], spl[:, 0:nq_t]),
                           waits=[t2], sig=None)
                    VTT(spl[:, 0:nq_t], spl[:, 0:nq_t], adt[:, 0:nq_t], ALU.add,
                        sig=None)
                    VTS(dtv[:, q0:q0 + nq_t], spl[:, 0:nq_t], 100.0, None,
                        ALU.min)
                    t4 = VTS(adt[:, 0:nq_t], dtv[:, q0:q0 + nq_t], acol[:, 0:1],
                             None, ALU.mult, waits=[tk_acol])
                    nch = nq_t // 128
                    for z_ in range(nch):
                        t4 = P.emit("vector",
                                    lambda e, z_=z_, q0=q0: e.tensor_tensor_scan(
                                        acum[:, q0 + z_ * 128:
                                             q0 + (z_ + 1) * 128],
                                        adt[:, z_ * 128:(z_ + 1) * 128],
                                        adt[:, z_ * 128:(z_ + 1) * 128],
                                        0.0, ALU.add, ALU.bypass),
                                    sig="ve")
                    dt_done = t4
                    dt_prep_toks.append(
                        P.dma("sync", "acw", acum_s[:, q0:q0 + nq_t],
                              acum[:, q0:q0 + nq_t], waits=[dt_done]))
                    for z_ in range(nch):
                        qz = q0 + z_ * 128
                        t5 = ACT(spl[:, 0:128], acum[:, qz:qz + 128], AF.Exp,
                                 bias=acum[:, qz + 127:qz + 128], scale=-1.0,
                                 waits=[dt_done, last_spl_u])
                        t6 = TR(bank(7, 128), spl[:, 0:128], idf[:, :],
                                waits=[t5, tk_b7])
                        last_spl_u = t6
                        t6 = VCOPY(dsT_all[:, qz:qz + 128], bank(7, 128),
                                   waits=[t6])
                        t7 = TR(bank(7, 128), dtv[:, qz:qz + 128], idf[:, :],
                                waits=[t6])
                        t7 = VCOPY(dtT_all[:, qz:qz + 128], bank(7, 128),
                                   waits=[t7])
                        t8 = TR(bank(7, 128), acum[:, qz:qz + 128], idf[:, :],
                                waits=[t7])
                        t8 = VTS(nacT_all[:, qz:qz + 128], bank(7, 128), -1.0,
                                 None, ALU.mult, waits=[t8])
                        t9 = TR(bank(7, 128)[0:1, 0:128],
                                acum[:, qz + 127:qz + 128],
                                idf[:, :], waits=[t8])
                        t9 = ACT(arow_all[0:1, qz:qz + 128],
                                 bank(7, 128)[0:1, 0:128],
                                 AF.Exp, waits=[t9])
                        tk_b7 = t9
                        dt_prep_toks.append(t9)

        p1_done = [mm_tok, last_epil[0], last_epil[1], last_xsil_u[0],
                   last_xsil_u[1], last_cscr_u[0], last_cscr_u[1],
                   last_tstg_u[0], last_tstg_u[1], tk_b7] + dt_prep_toks
        AR.release()
        if upto == "p1":
            P.emit("sync", lambda e: None,
                   waits=p1_done + P.fence(*[k[2:] for k in P.val if k.startswith("D_")]))
            return _finish(P, block, nc)

        AR.mark()
        prevT = AR.f32(INTER)
        prevB = AR.bf16(INTER)
        hst = [AR.bf16(INTER) for _ in range(2)]
        bcB = [AR.bf16(NG * 128) for _ in range(2)]
        bcC = [AR.bf16(NG * 128) for _ in range(2)]
        dmw = [AR.f32(512) for _ in range(2)]
        dmv = [AR.f32(512) for _ in range(2)]
        ltb = [AR.bf16(512) for _ in range(2)]
        ea2 = [AR.bf16(512) for _ in range(2)]
        mtb = [AR.bf16(512) for _ in range(2)]
        cet = [AR.bf16(512) for _ in range(2)]
        xchb = [AR.bf16(256) for _ in range(4)]
        xchd = [AR.bf16(256) for _ in range(4)]
        xchD = [AR.bf16(256) for _ in range(4)]
        prec = [AR.f32(512) for _ in range(2)]

        NB = 32
        tk_zero = P.emit("vector", lambda e: e.memzero(prevT[:, :]),
                         waits=p1_done, sig="ve")
        last_hst_u = [p1_done, p1_done]
        last_bc_u = [p1_done, p1_done]
        last_rb = [p1_done, p1_done, p1_done]
        last_y = [p1_done, p1_done]
        last_s = [p1_done, p1_done]
        last_dm = [p1_done, p1_done]
        last_lt = [None, None]
        last_mt = [None, None]
        last_ct = [None, None]
        last_xc = [tk_zero] * 4
        last_af = [None, None]
        last_gb = [None, None]
        last_ys = [None, None]
        last_prec = [None, None]
        y_toks = [None] * max(n_main, 1)
        tk_prev_chain = tk_zero
        cur_af = None
        tk_gsb = None

        for zz in range(NZ_RUN):
            is_main = zz >= NPREF
            z = zz - NPREF
            hb2 = zz % 2
            t_hst = P.dma("sync", f"hst{hb2}", hst[hb2][:, :],
                          hsT_s[zz * 128:(zz + 1) * 128, :],
                          waits=[last_hst_u[hb2]] + (hs_toks if zz <= NPREF else []))
            t_bcb, t_bcc = [], []
            for g in range(NG):
                t_bcb.append(P.dma("scalar", f"bc{hb2}", bcB[hb2][:, g * 128:(g + 1) * 128],
                                   b_s[g * 128:(g + 1) * 128,
                                       zz * 128:(zz + 1) * 128],
                                   waits=[last_bc_u[hb2]] + (bc_toks if zz <= NPREF else [])))
                if is_main:
                    t_bcc.append(P.dma("scalar", f"bc{hb2}",
                                       bcC[hb2][:, g * 128:(g + 1) * 128],
                                       c_s[g * 128:(g + 1) * 128,
                                           zz * 128:(zz + 1) * 128],
                                       waits=[last_bc_u[hb2]] + (bc_toks if zz <= NPREF else [])))
            t_am = MM(bank(7, 128), ones_f[0:1, :],
                      arow_all[0:1, zz * 128: zz * 128 + 128], True, True,
                      waits=[tk_b7] + dt_prep_toks)
            t_am = VCOPY(amat[:, :], bank(7, 128), waits=[t_am])
            tk_b7 = t_am
            tk_prevB = VCOPY(prevB[:, :], prevT[:, :], waits=[tk_prev_chain])

            for hb in range(NB):
                g = hb // 4
                i4 = hb % 4
                rot2 = hb % 2
                rot3 = hb % 3
                rot4 = hb % 4
                ybk = (hb // 2) % 2
                sbk = 2 + (hb // 2) % 2
                h0 = hb * 4
                gpar = g % 2

                if i4 == 0:
                    if is_main:
                        t_g = MM(bank(7, 128), bcB[hb2][:, g * 128:(g + 1) * 128],
                                 bcC[hb2][:, g * 128:(g + 1) * 128], True, True,
                                 waits=[tk_b7, t_bcb[g], t_bcc[g]])
                        t_bt = TR(bwin(7, 128, 256),
                                  bcB[hb2][:, g * 128:(g + 1) * 128],
                                  idb[:, :], waits=[])
                        t_g2 = VCOPY(gsb[gpar][:, :], bank(7, 128),
                                     waits=[t_bt, last_gb[gpar]])
                        t_b2 = VCOPY(btk[gpar][:, :], bwin(7, 128, 256),
                                     waits=[])
                        tk_b7 = t_b2
                        tk_gsb = t_g2
                        tk_btk = t_b2
                    else:
                        t_bt = TR(bwin(7, 128, 256),
                                  bcB[hb2][:, g * 128:(g + 1) * 128],
                                  idb[:, :], waits=[tk_b7, t_bcb[g]])
                        t_b2 = VCOPY(btk[gpar][:, :], bwin(7, 128, 256),
                                     waits=[t_bt, last_gb[gpar]])
                        tk_b7 = t_b2
                        tk_btk = t_b2

                hsl = hst[hb2][:, h0 * 64:(h0 + 4) * 64]
                t_xc = VTT(r3(xchb[rot4][:, :], 64), r3(hsl, 64),
                           bfree(dtT_all[:, zz * 128 + h0: zz * 128 + h0 + 4],
                                 4, 64),
                           ALU.mult, waits=[t_hst, last_xc[rot4]], sig="ve",
                           eng="gpsimd")
                t_xd = VTT(r3(xchd[rot4][:, :], 64), r3(xchb[rot4][:, :], 64),
                           bfree(dsT_all[:, zz * 128 + h0: zz * 128 + h0 + 4],
                                 4, 64),
                           ALU.mult, sig="ve", eng="gpsimd")
                if is_main:
                    t_xD = VTT(r3(xchD[rot4][:, :], 64), r3(hsl, 64),
                               bfree(drep[:, h0:h0 + 4], 4, 64),
                               ALU.mult, waits=[tk_drep], sig="ve",
                               eng="gpsimd")
                else:
                    t_xD = t_xd

                if is_main and FLVL >= 2:
                    if hb % 2 == 0:
                        apar = (hb // 2) % 2
                        cur_af = P.dma("scalar", f"af{apar}",
                                       afb[apar][0:1, :].rearrange(
                                           "p (h l) -> p h l", l=128),
                                       acum_s[h0:h0 + 8,
                                              zz * 128:(zz + 1) * 128].unsqueeze(0),
                                       waits=[last_af[apar]] + dt_prep_toks)
                        last_af[apar] = cur_af
                    rbk = 4 + rot3
                    t_rb = MM(bank(rbk), ones_f[0:1, :],
                              afb[(hb // 2) % 2][0:1,
                                                 (hb % 2) * 512:(hb % 2) * 512 + 512],
                              True, True, waits=[cur_af, last_rb[rot3]])
                    if FLVL < 3:
                        continue
                    t_da = VTT(r3(dmw[rot2][:, :], 128), r3(bank(rbk), 128),
                               bmid(trim[:, :], 4), ALU.add,
                               waits=[t_rb, last_dm[rot2]])
                    t_db = VTT(r3(dmv[rot2][:, :], 128), r3(dmw[rot2][:, :], 128),
                               bfree(nacT_all[:, zz * 128 + h0:
                                              zz * 128 + h0 + 4], 4, 128),
                               ALU.add, waits=[last_lt[rot2]])
                    t_ea = ACT(ea2[rot2][:, :], bank(rbk), AF.Exp,
                               waits=[t_rb, last_ct[rot2], t_da])
                    last_rb[rot3] = [t_da, t_ea]
                    t_lt = ACT(ltb[rot2][:, :], dmv[rot2][:, :], AF.Exp,
                               waits=[t_db, last_mt[rot2]])
                    last_dm[rot2] = t_lt
                    if FLVL < 4:
                        continue
                    t_mt = VTT(r3(mtb[rot2][:, :], 128), r3(ltb[rot2][:, :], 128),
                               bmid(gsb[gpar][:, :], 4), ALU.mult,
                               waits=[t_lt, tk_gsb])
                    last_lt[rot2] = t_mt
                    t_ct = VTT(r3(cet[rot2][:, :], 128),
                               bmid(bcC[hb2][:, g * 128:(g + 1) * 128], 4),
                               r3(ea2[rot2][:, :], 128), ALU.mult,
                               waits=[t_ea, t_bcc[g]])
                    last_ct[rot2] = t_ct
                    t_y = None
                    if FLVL < 5:
                        continue
                    for i in range(4):
                        h = h0 + i
                        o_y = bank(ybk)[:, (h % 8) * 64:(h % 8) * 64 + 64]
                        MM(o_y, mtb[rot2][:, i * 128:(i + 1) * 128],
                           xchb[rot4][:, i * 64:(i + 1) * 64], True, False,
                           waits=[t_mt, t_xc,
                                  last_y[ybk] if (hb % 2 == 0 and i == 0)
                                  else None],
                           sig=None)
                        MM(o_y, cet[rot2][:, i * 128:(i + 1) * 128],
                           prevB[:, h * 64:(h + 1) * 64], False, False,
                           waits=[t_ct, tk_prevB], sig=None)
                        t_y = MM(o_y, idb[:, :],
                                 xchD[rot4][:, i * 64:(i + 1) * 64],
                                 False, True, waits=[t_xD],
                                 sig="pe" if i == 3 else None)
                    last_mt[rot2] = t_y
                    last_ct[rot2] = [t_ct, t_y]
                else:
                    t_y = None

                t_st = None
                for i in range(4):
                    h = h0 + i
                    o_s = bank(sbk)[:, (h % 8) * 64:(h % 8) * 64 + 64]
                    t_st = MM(o_s, btk[gpar][:, :],
                              xchd[rot4][:, i * 64:(i + 1) * 64],
                              True, True,
                              waits=[tk_btk, t_xd,
                                     last_s[sbk - 2] if (hb % 2 == 0 and i == 0)
                                     else None],
                              sig="pe" if i == 3 else None)
                last_gb[gpar] = t_y if is_main else t_st
                last_xc[rot4] = t_st

                if hb % 2 == 1:
                    h8 = (hb // 2) * 8
                    ypar = (hb // 2) % 2
                    if is_main and FLVL >= 6:
                        t_ye = VCOPY(ystg[ypar][:, :], bank(ybk),
                                     waits=[t_y, last_ys[ypar]])
                        last_y[ybk] = t_ye
                        t_yd = P.dma("sync", f"ys{ypar}",
                                     y_s[z * 128:(z + 1) * 128,
                                         h8 * 64: h8 * 64 + 512],
                                     ystg[ypar][:, :], waits=[t_ye])
                        last_ys[ypar] = t_yd
                        if hb == NB - 1:
                            y_toks[z] = t_yd
                    ppar = (hb // 2) % 2
                    t_p1 = VTT(r3(prec[ppar][:, :], 64),
                               r3(prevT[:, h8 * 64: h8 * 64 + 512], 64),
                               bfree(amat[:, h8:h8 + 8], 8, 64), ALU.mult,
                               waits=[t_am, tk_prevB, last_prec[ppar]])
                    t_p2 = VTT(prevT[:, h8 * 64: h8 * 64 + 512],
                               prec[ppar][:, :],
                               bank(sbk), ALU.add, waits=[t_st, t_p1])
                    last_prec[ppar] = t_p2
                    last_s[sbk - 2] = t_p2
                    tk_prev_chain = t_p2

            last_hst_u[hb2] = last_xc
            last_bc_u[hb2] = tk_prev_chain
            if zz == NPREF - 1:
                tk_prev_chain = VTS(prevT[:, :], prevT[:, :], smk[:, 0:1], None,
                                    ALU.mult, waits=[tk_prev_chain])

        p2_done = [tk_prev_chain, tk_b7, tk_prevB] + last_hst_u[0] + \
            last_hst_u[1] + _tl(last_ys) + _tl(last_y) + _tl(last_s) + \
            _tl(last_mt) + _tl(last_ct) + _tl(last_dm) + _tl(last_prec)
        AR.release()
        if upto in ("p2", "p2pre", "p2one"):
            P.emit("sync", lambda e: None,
                   waits=p2_done + P.fence(*[k[2:] for k in P.val if k.startswith("D_")]))
            return _finish(P, block, nc)

        AR.mark()
        ybuf = AR.bf16(INTER)
        sgbuf = AR.bf16(INTER)
        g_toks = []
        tk_gst = [p2_done, p2_done]
        t_gn_last = p2_done
        for z in range(n_main):
            t_y = P.dma("sync", "yr", ybuf[:, :], y_s[z * 128:(z + 1) * 128, :],
                        waits=P.fence("ys0", "ys1") + [t_gn_last])
            t_sg = P.dma("sync", "sgr", sgbuf[:, :], sg_s[z * 128:(z + 1) * 128, :],
                         waits=(P.fence("tt0", "tt1") if z == 0 else []) + [t_gn_last])
            t_g = VTT(ybuf[:, :], ybuf[:, :], sgbuf[:, :], ALU.mult,
                      waits=[t_y, t_sg])
            t_sq = None
            for grp in range(8):
                t_sq = ACT(sgbuf[:, 0:2048].bitcast(F32)[:, 0:1024],
                           ybuf[:, grp * 1024:(grp + 1) * 1024],
                           AF.Square, waits=[t_g], accum=ssq[:, grp:grp + 1])
            t_v = VTS(rstd8[:, :], ssq[:, :], 1.0 / 1024.0, EPS, ALU.mult,
                      ALU.add, waits=[t_sq])
            t_v = ACT(rstd8[:, :], rstd8[:, :], AF.Ln, waits=[t_v])
            t_v = ACT(rstd8[:, :], rstd8[:, :], AF.Exp, scale=-0.5)
            t_v = VCOPY(rstdb[:, :], rstd8[:, :], waits=[t_v])
            t_gn = VTT(r3(ybuf[:, :], 1024), r3(ybuf[:, :], 1024),
                       bfree(rstdb[:, :], 8, 1024), ALU.mult, waits=[t_v])
            t_gn_last = t_gn
            dtok = None
            for ct in range(64):
                slot = ct % 2
                o_ps = bwin(slot, 128)
                t_t = TR(o_ps, ybuf[:, ct * 128:(ct + 1) * 128], idb[:, :],
                         waits=[t_gn, tk_gst[slot]])
                ev = P.emit("vector",
                            lambda e, slot=slot, ct=ct, o_ps=o_ps:
                            e.tensor_scalar(gstg[slot][:, :], o_ps,
                                            nwcs[:, ct:ct + 1], None, ALU.mult),
                            waits=[t_t], sig="ve")
                dtok = P.dma("sync", f"gw{slot}",
                             g_s[ct * 128:(ct + 1) * 128, z * 128:(z + 1) * 128],
                             gstg[slot][:, :], waits=[ev])
                tk_gst[slot] = dtok
            g_toks.append(dtok)
        p25_done = _tl(tk_gst) + [t_gn_last]
        AR.release()
        if upto == "p25":
            P.emit("sync", lambda e: None,
                   waits=p25_done + P.fence(*[k[2:] for k in P.val if k.startswith("D_")]))
            return _finish(P, block, nc)

        AR.mark()
        x1ct = r3(AR.bf16(KD * 128), 128)
        x1pt = AR.f32(HID)
        x1bt = AR.bf16(HID)
        stat_sc = AR.f32(HID)
        aos = AR.f32(4 * 512)
        outt = AR.f32(HID)
        hwT = AR.bf16(NM)
        wof = AR.bf16(4 * 64 * 128)
        wob = [r3(wof[:, j * 64 * 128:(j + 1) * 64 * 128], 128) for j in range(4)]
        wbb = [r3(wof[:, j * 32 * 128:(j + 1) * 32 * 128]
                  if False else wof[:, (4 * 64 * 128 // 2) + j * 32 * 128:
                                    (4 * 64 * 128 // 2) + (j + 1) * 32 * 128], 128)
               for j in range(4)]

        last_wob = [p25_done] * 4
        last_gcol = [p25_done, p25_done]
        last_aos = [p25_done] * 4
        last_xb = [None, None]
        last_x1s = [None, None]
        bank_free = [p25_done] * 8
        x1p_toks = []
        for ob in range(8):
            t_w = []
            for j in range(4):
                t_w.append(P.dma("gpsimd", "wobl", wob[j][:, :, :], wot[ob * 4 + j],
                                 waits=[last_wob[j]]))
            mm_last = [None] * 4
            for ci in range(NCOL):
                cw = min(512, NM - ci * 512)
                cpar = ci % 2
                for k in range(64):
                    gp = k % 2
                    t_gc = P.dma("sync", f"gc{gp}", gcol[gp][:, 0:cw],
                                 g_s[k * 128:(k + 1) * 128,
                                     ci * 512: ci * 512 + cw],
                                 waits=[last_gcol[gp]] +
                                 (P.fence("gw0", "gw1")
                                  if (ob == 0 and ci == 0 and k < 2) else []))
                    for ot in range(4):
                        bk = ot + 4 * cpar
                        mm_last[ot] = MM(bank(bk, cw), wob[ot][:, k, :],
                                         gcol[gp][:, 0:cw], (k == 0), (k == 63),
                                         waits=[t_gc, t_w[ot],
                                                bank_free[bk] if k == 0
                                                else None],
                                         sig="pe" if ot == 3 else None)
                    last_gcol[gp] = mm_last[3]
                for ot in range(3):
                    mm_last[ot] = mm_last[3]
                for ot in range(4):
                    bk = ot + 4 * cpar
                    oidx = ob * 4 + ot
                    t_e = ACT(aos[:, ot * 512: ot * 512 + cw], bank(bk, cw),
                              AF.Copy, waits=[mm_last[ot], last_aos[ot]])
                    nsub = cw // 128
                    t_tr = None
                    for sub in range(nsub):
                        t_tr = TR(bank(bk, 128, sub * 128),
                                  aos[:, ot * 512 + sub * 128:
                                      ot * 512 + sub * 128 + 128],
                                  idf[:, :], waits=[t_e],
                                  sig="pe" if sub == nsub - 1 else None)
                    last_aos[ot] = t_tr
                    dlast = None
                    for sub in range(nsub):
                        xpar = sub % 2
                        mt0 = ci * 512 + sub * 128
                        t_xb = P.dma("scalar", f"xb{xpar}", xblk[xpar][:, :],
                                     xin[PAD + mt0: PAD + mt0 + 128,
                                         oidx * 128:(oidx + 1) * 128],
                                     waits=[last_xb[xpar]])
                        t_ad = VTT(x1stg[xpar][:, :], bank(bk, 128, sub * 128),
                                   xblk[xpar][:, :], ALU.add,
                                   waits=[t_tr, t_xb, last_x1s[xpar]])
                        last_xb[xpar] = t_ad
                        dlast = P.dma("sync", f"x1w{xpar}",
                                      x1p_s[mt0:mt0 + 128,
                                            oidx * 128:(oidx + 1) * 128],
                                      x1stg[xpar][:, :], waits=[t_ad])
                        last_x1s[xpar] = dlast
                    bank_free[bk] = dlast
            for j in range(4):
                last_wob[j] = mm_last[j]
            x1p_toks.append(dlast)

        if upto == "p3":
            P.emit("sync", lambda e: None,
                   waits=_tl(x1p_toks) + P.fence(*[k[2:] for k in P.val if k.startswith("D_")]))
            AR.release()
            return _finish(P, block, nc)
        last_x1T = x1p_toks + _tl(tk_gst[0]) + _tl(tk_gst[1])
        x1T_toks = []
        tk_x1pt = x1p_toks
        tk_gstg = [last_x1T, last_x1T]
        for i in range(NTT):
            t_l = P.dma("sync", "x1r", x1pt[:, :], x1p_s[i * 128:(i + 1) * 128, :],
                        waits=(P.fence("x1w0", "x1w1") if i == 0 else []) +
                        _tl(tk_x1pt))
            t_mu = VTS(stat_sc[:, :], x1pt[:, :], 1.0 / HID, 0.0, ALU.mult,
                       ALU.add, waits=[t_l], accum=mu1[:, i:i + 1])
            t_s2 = ACT(stat_sc[:, :], x1pt[:, :], AF.Square, scale=1.0 / 64.0,
                       waits=[t_mu], accum=exc[:, 0:1])
            t_v = VTT(vtmp[:, 0:1], mu1[:, i:i + 1], mu1[:, i:i + 1], ALU.mult,
                      waits=[t_mu])
            t_v = VTT(vtmp[:, 0:1], exc[:, 0:1], vtmp[:, 0:1], ALU.subtract,
                      waits=[t_s2])
            t_v = VTS(vtmp[:, 0:1], vtmp[:, 0:1], EPS, None, ALU.add)
            t_v = ACT(vtmp[:, 0:1], vtmp[:, 0:1], AF.Ln, waits=[t_v])
            t_v = ACT(rs1[:, i:i + 1], vtmp[:, 0:1], AF.Exp, scale=-0.5)
            t_n = VTT(nmr1[:, i:i + 1], mu1[:, i:i + 1], rs1[:, i:i + 1],
                      ALU.mult, waits=[t_v])
            t_n = VTS(nmr1[:, i:i + 1], nmr1[:, i:i + 1], -1.0, None, ALU.mult)
            t_x1 = P.emit("vector",
                          lambda e, i=i: e.tensor_scalar(
                              x1bt[:, :], x1pt[:, :], rs1[:, i:i + 1],
                              nmr1[:, i:i + 1], ALU.mult, ALU.add),
                          waits=[t_n], sig="ve")
            tk_x1pt = t_x1
            ev = None
            for k in range(KD):
                slot = k % 2
                o_ps = bwin(slot, 128)
                t_t = TR(o_ps, x1bt[:, k * 128:(k + 1) * 128], idb[:, :],
                         waits=[t_x1, tk_gstg[slot]])
                ev = VCOPY(gstg[slot][:, :], o_ps, waits=[t_t])
                ev = P.dma("sync", f"gw{slot}", x1t_s[k * 128:(k + 1) * 128,
                                         i * 128:(i + 1) * 128],
                           gstg[slot][:, :], waits=[ev])
                tk_gstg[slot] = ev
            x1T_toks.append(ev)

        alts = AR.bf16(KD * ER)
        blts = AR.bf16(HID)
        t_alt = P.dma("gpsimd", "iniw", alts.rearrange("p (k e) -> p k e", e=ER), alt)
        t_blt = P.dma("gpsimd", "iniw", blts, blt)

        if upto == "p3t":
            P.emit("sync", lambda e: None,
                   waits=_tl(x1T_toks) + P.fence(*[k[2:] for k in P.val if k.startswith("D_")]))
            AR.release()
            return _finish(P, block, nc)
        hw_toks = []
        last_hw = [None, None]
        tk_r = x1T_toks[-1]
        tk_x1ct = None
        for i in range(NTT):
            t_xc1 = P.dma("sync", "x1ct", x1ct[:, :, :],
                          x1t_s[:, i * 128:(i + 1) * 128].rearrange(
                              "(k p) t -> p k t", p=128),
                          waits=P.fence("gw0", "gw1") + [tk_x1ct])
            mmr = None
            for k in range(KD):
                mmr = MM(bank(0, 8), x1ct[:, k, :],
                         wrts[:, k * 8:(k + 1) * 8], (k == 0), (k == KD - 1),
                         waits=[t_xc1, tk_r], sig="pe" if k == KD - 1 else None)
            t_m1 = P.emit("vector",
                          lambda e: e.reduce_max(m1c[:, 0:1], bank(0, 8),
                                                 axis=AX.X),
                          waits=[mmr], sig="ve")
            t_m1 = VTS(m1c[:, 0:1], m1c[:, 0:1], -1.0, None, ALU.mult)
            t_z = ACT(zrow[:, :], bank(0, 8), AF.Exp, bias=m1c[:, 0:1],
                      waits=[t_m1])
            t_q = VTS(qrow[:, :], zrow[:, :], 1.0, None, ALU.is_lt, waits=[t_z])
            t_q = VTT(qrow[:, :], qrow[:, :], zrow[:, :], ALU.mult)
            t_m2 = P.emit("vector",
                          lambda e: e.reduce_max(m2c[:, 0:1], qrow[:, :],
                                                 axis=AX.X),
                          waits=[t_q], sig="ve")
            t_d = VTS(denc[:, 0:1], m2c[:, 0:1], 1.0, None, ALU.add,
                      waits=[t_m2])
            t_d = P.emit("vector",
                         lambda e: e.reciprocal(denc[:, 0:1], denc[:, 0:1]),
                         sig="ve")
            t_mk = VTS(qrow[:, :], zrow[:, :], m2c[:, 0:1], None, ALU.is_ge,
                       waits=[t_m2])
            t_w2 = VTT(zrow[:, :], zrow[:, :], qrow[:, :], ALU.mult,
                       waits=[t_mk])
            t_cb = VTS(combt[:, :], zrow[:, :], denc[:, 0:1], None, ALU.mult,
                       waits=[t_d, t_w2])
            mml = None
            for k in range(KD):
                mml = MM(bank(1, 128), x1ct[:, k, :],
                         alts[:, k * ER:(k + 1) * ER], (k == 0), (k == KD - 1),
                         waits=[t_alt, t_cb if k == 0 else None],
                         sig="pe" if k == KD - 1 else None)
            tk_x1ct = mml
            hpar = i % 2
            t_hw = P.emit("vector",
                          lambda e, hpar=hpar: e.scalar_tensor_tensor(
                              r3(hwstg[hpar][:, :], 16), r3(bank(1, 128), 16),
                              2.0, bfree(combt[:, :], 8, 16),
                              ALU.mult, ALU.mult),
                          waits=[mml, last_hw[hpar]], sig="ve")
            o_ps = bwin(2, 128)
            t_t = TR(o_ps, hwstg[hpar][:, :], idb[:, :], waits=[t_hw])
            t_e = VCOPY(hwT[:, i * 128:(i + 1) * 128], o_ps, waits=[t_t])
            last_hw[hpar] = t_e
            tk_r = t_e
            hw_toks.append(t_e)

        if upto == "p4a":
            P.emit("sync", lambda e: None,
                   waits=_tl(hw_toks) + P.fence(*[k[2:] for k in P.val if k.startswith("D_")]))
            AR.release()
            return _finish(P, block, nc)
        last_wbb = [[hw_toks[-1]] + _tl(last_wob)] * 4
        last_ffs = [None, None]
        last_xb2 = [None, None]
        last_aos = [[hw_toks[-1]]] * 4
        bank_free = [[hw_toks[-1]]] * 8
        ffp_toks = []
        for ob in range(8):
            t_w = []
            for j in range(4):
                t_w.append(P.dma("gpsimd", "wobl", wbb[j][:, :, :], wbt[ob * 4 + j],
                                 waits=[last_wbb[j]]))
            mm_last = [None] * 4
            for ci in range(NCOL):
                cw = min(512, NM - ci * 512)
                cpar = ci % 2
                for k in range(32):
                    gp = k % 2
                    t_xc2 = P.dma("sync", f"gc{gp}", gcol[gp][:, 0:cw],
                                  x1t_s[k * 128:(k + 1) * 128,
                                        ci * 512: ci * 512 + cw],
                                  waits=[last_gcol[gp]] +
                                  (P.fence("gw0", "gw1")
                                   if (ob == 0 and ci == 0 and k < 2) else []))
                    tkk = None
                    for ot in range(4):
                        bk = ot + 4 * cpar
                        tkk = MM(bank(bk, cw), wbb[ot][:, k, :],
                                 gcol[gp][:, 0:cw], (k == 0), False,
                                 waits=[t_xc2, t_w[ot],
                                        bank_free[bk] if k == 0 else None],
                                 sig="pe" if ot == 3 else None)
                    last_gcol[gp] = tkk
                for ot in range(4):
                    bk = ot + 4 * cpar
                    mm_last[ot] = MM(bank(bk, cw),
                                     blts[:, (ob * 4 + ot) * 128:
                                          (ob * 4 + ot + 1) * 128],
                                     hwT[:, ci * 512: ci * 512 + cw],
                                     False, True,
                                     waits=[t_blt, hw_toks[-1]], sig="pe")
                for ot in range(4):
                    bk = ot + 4 * cpar
                    oidx = ob * 4 + ot
                    t_e = P.emit("vector",
                                 lambda e, bk=bk, cw=cw, ot=ot, oidx=oidx:
                                 e.tensor_scalar(aos[:, ot * 512: ot * 512 + cw],
                                                 bank(bk, cw),
                                                 bbcs[:, oidx:oidx + 1],
                                                 None, ALU.add),
                                 waits=[mm_last[ot], last_aos[ot]], sig="ve")
                    nsub = cw // 128
                    t_tr = None
                    for sub in range(nsub):
                        t_tr = TR(bank(bk, 128, sub * 128),
                                  aos[:, ot * 512 + sub * 128:
                                      ot * 512 + sub * 128 + 128],
                                  idf[:, :], waits=[t_e],
                                  sig="pe" if sub == nsub - 1 else None)
                    last_aos[ot] = t_tr
                    dlast = None
                    for sub in range(nsub):
                        xpar = sub % 2
                        mt0 = ci * 512 + sub * 128
                        tt = mt0 // 128
                        t_xb = P.dma("scalar", f"xb{xpar}", xblk[xpar][:, :],
                                     x1p_s[mt0:mt0 + 128,
                                           oidx * 128:(oidx + 1) * 128],
                                     waits=[last_xb2[xpar]] +
                                     (P.fence("x1w0", "x1w1")
                                      if last_xb2[xpar] is None else []))
                        t_a1 = P.emit("vector",
                                      lambda e, xpar=xpar, tt=tt:
                                      e.tensor_scalar(
                                          xblk[xpar][:, :], xblk[xpar][:, :],
                                          rs1[:, tt:tt + 1], nmr1[:, tt:tt + 1],
                                          ALU.mult, ALU.add),
                                      waits=[t_xb], sig="ve")
                        t_a2 = VTT(ffstg[xpar][:, :], bank(bk, 128, sub * 128),
                                   xblk[xpar][:, :], ALU.add,
                                   waits=[t_tr, t_a1, last_ffs[xpar]])
                        last_xb2[xpar] = t_a2
                        dlast = P.dma("sync", f"x1w{xpar}",
                                      ffp_s[mt0:mt0 + 128,
                                            oidx * 128:(oidx + 1) * 128],
                                      ffstg[xpar][:, :], waits=[t_a2])
                        last_ffs[xpar] = dlast
                    bank_free[bk] = dlast
            for j in range(4):
                last_wbb[j] = mm_last[j]
            ffp_toks.append(dlast)

        if upto == "p4b":
            P.emit("sync", lambda e: None,
                   waits=_tl(ffp_toks) + P.fence(*[k[2:] for k in P.val if k.startswith("D_")]))
            AR.release()
            return _finish(P, block, nc)
        out_toks = []
        tk_o = ffp_toks
        for i in range(NTT):
            t_l = P.dma("sync", "x1r", x1pt[:, :], ffp_s[i * 128:(i + 1) * 128, :],
                        waits=(P.fence("x1w0", "x1w1") if i == 0 else []) +
                        _tl(tk_o))
            t_mu = VTS(stat_sc[:, :], x1pt[:, :], 1.0 / HID, 0.0, ALU.mult,
                       ALU.add, waits=[t_l], accum=mu1[:, i:i + 1])
            t_s2 = ACT(stat_sc[:, :], x1pt[:, :], AF.Square, scale=1.0 / 64.0,
                       waits=[t_mu], accum=exc[:, 0:1])
            t_v = VTT(vtmp[:, 0:1], mu1[:, i:i + 1], mu1[:, i:i + 1], ALU.mult,
                      waits=[t_mu])
            t_v = VTT(vtmp[:, 0:1], exc[:, 0:1], vtmp[:, 0:1], ALU.subtract,
                      waits=[t_s2])
            t_v = VTS(vtmp[:, 0:1], vtmp[:, 0:1], EPS, None, ALU.add)
            t_v = ACT(vtmp[:, 0:1], vtmp[:, 0:1], AF.Ln, waits=[t_v])
            t_v = ACT(rs1[:, i:i + 1], vtmp[:, 0:1], AF.Exp, scale=-0.5)
            t_n = VTT(nmr1[:, i:i + 1], mu1[:, i:i + 1], rs1[:, i:i + 1],
                      ALU.mult, waits=[t_v])
            t_n = VTS(nmr1[:, i:i + 1], nmr1[:, i:i + 1], -1.0, None, ALU.mult)
            t_o = P.emit("vector",
                         lambda e, i=i: e.tensor_scalar(
                             outt[:, :], x1pt[:, :], rs1[:, i:i + 1],
                             nmr1[:, i:i + 1], ALU.mult, ALU.add),
                         waits=[t_n] + (out_toks[-1:] if out_toks else []),
                         sig="ve")
            tk_o = t_o
            out_toks.append(P.dma("sync", "outw", out[i * 128:(i + 1) * 128, :],
                                  outt[:, :], waits=[t_o]))
        AR.release()

        P.emit("sync", lambda e: None, waits=[out_toks[-1]])
        return _finish(P, block, nc)


def _prep_w_tiles(w):
    o, i = w.shape
    e, kk = o // 128, i // 128
    return np.ascontiguousarray(
        w.reshape(e, 128, kk, 128).transpose(0, 3, 2, 1))


def make_inputs(x, weights, n_main):
    f = np.float32
    NM = n_main * CHUNK
    NR = PAD + NM
    xpad = np.concatenate([np.zeros((PAD, HID), f), x], 0)
    wint = _prep_w_tiles(np.asarray(weights["in_proj_w"], f))
    wotm = _prep_w_tiles(np.asarray(weights["out_proj_w"], f))
    wbtm = _prep_w_tiles(np.asarray(weights["W_base"], f))
    cw = np.asarray(weights["conv_w"], f).reshape(NE_HBC, 128, KCV)
    convw = np.ascontiguousarray(cw.transpose(1, 0, 2).reshape(128, NE_HBC * KCV))
    convb = np.ascontiguousarray(
        np.asarray(weights["conv_b"], f).reshape(NE_HBC, 128).T)
    nwcm = np.ascontiguousarray(
        np.asarray(weights["norm_w"], f).reshape(64, 128).T)
    bbcm = np.ascontiguousarray(
        np.asarray(weights["b_base"], f).reshape(32, 128).T)
    dtbc = np.asarray(weights["dt_bias"], f).reshape(128, 1)
    alogc = np.asarray(weights["A_log"], f).reshape(128, 1)
    drowm = np.asarray(weights["D"], f).reshape(1, 128)
    wrtm = np.ascontiguousarray(
        np.asarray(weights["W_router"], f).T.reshape(KD, 128, 8).transpose(1, 0, 2))
    altm = np.ascontiguousarray(
        np.asarray(weights["A"], f).reshape(ER, HID).T.reshape(KD, 128, ER)
        .transpose(1, 0, 2))
    bltm = np.ascontiguousarray(
        np.asarray(weights["B"], f).transpose(0, 2, 1).reshape(ER, HID))
    identf = np.eye(128, dtype=f)
    identb = np.eye(128).astype(ml_dtypes.bfloat16)
    tri = np.where(np.arange(128)[:, None] > np.arange(128)[None, :], NEG, 0.0)
    trimask = np.ascontiguousarray(tri.astype(f))
    onesrow = np.ones((1, 128), f)
    ncores = x.shape[0] // NM
    in_maps = []
    for c in range(ncores):
        r0 = c * NM
        smaskv = np.full((128, 1), 0.0 if c == 0 else 1.0, f)
        in_maps.append(dict(
            xin=np.ascontiguousarray(xpad[r0:r0 + NR]),
            wint=wint, wot=wotm, wbt=wbtm, convw=convw, convb=convb,
            nwc=nwcm, bbc=bbcm, dtbc=dtbc, alogc=alogc, drow=drowm,
            wrt=wrtm, alt=altm, blt=bltm, identf=identf, identb=identb,
            trimask=trimask, onesrow=onesrow, smask=smaskv))
    return in_maps


def kernel(L0_attn_x, L0_attn_in_proj_w, L0_attn_conv_w, L0_attn_conv_b,
           L0_attn_dt_bias, L0_attn_A_log, L0_attn_D, L0_attn_norm_w,
           L0_attn_out_proj_w, L0_ffn_W_base, L0_ffn_b_base, L0_ffn_W_router,
           L0_ffn_A, L0_ffn_B):
    x = np.asarray(L0_attn_x, np.float32)[0]
    S = x.shape[0]
    n_main = S // NCORES // CHUNK
    weights = dict(in_proj_w=L0_attn_in_proj_w, conv_w=L0_attn_conv_w,
                   conv_b=L0_attn_conv_b, dt_bias=L0_attn_dt_bias,
                   A_log=L0_attn_A_log, D=L0_attn_D, norm_w=L0_attn_norm_w,
                   out_proj_w=L0_attn_out_proj_w, W_base=L0_ffn_W_base,
                   b_base=L0_ffn_b_base, W_router=L0_ffn_W_router,
                   A=L0_ffn_A, B=L0_ffn_B)
    in_maps = make_inputs(x, weights, n_main)
    nc = build_program(n_main=n_main)
    res = run_bass_kernel_spmd(nc, in_maps, core_ids=list(range(NCORES)))
    outp = np.concatenate([res.results[c]["out"] for c in range(NCORES)], 0)
    return outp[None, :, :].astype(np.float32)


# version 02 (final)
# speedup vs baseline: 1.1713x; 1.1713x over previous
import contextlib
import numpy as np
import ml_dtypes
import concourse.bass as bass
import concourse.mybir as mybir
from concourse.bass_utils import run_bass_kernel_spmd

F32 = mybir.dt.float32
BF16 = mybir.dt.bfloat16
AF = mybir.ActivationFunctionType
ALU = mybir.AluOpType
AX = mybir.AxisListType

HID = 4096
NH, PH, NS, NG, KCV = 128, 64, 128, 8, 4
INTER = NH * PH
CONVD = INTER + 2 * NG * NS
PROJ = INTER + CONVD + NH
CHUNK = 128
EPS = 1e-5
NCORES = 8
NPREF = 2
PAD = NPREF * CHUNK + 3
NE_GATE = INTER // 128
NE_HBC = CONVD // 128
NE = PROJ // 128
KD = HID // 128
R_LORA, NEXP = 16, 8
ER = NEXP * R_LORA
NEG = -3.0e38


def _tl(x):
    if x is None:
        return []
    if isinstance(x, list):
        out = []
        for e in x:
            out.extend(_tl(e))
        return out
    return [x]


class Prog:
    ENGS = ("tensor", "vector", "scalar", "gpsimd", "sync")

    def __init__(self, nc, ctx):
        self.nc = nc
        self.ctx = ctx
        self.q = {e: [] for e in self.ENGS}
        self.sems = {}
        self.val = {}
        self.waited = {e: {} for e in self.ENGS}
        for name in ("pe", "ve", "ae", "ge"):
            self.sems[name] = ctx.enter_context(nc.semaphore(name))
            self.val[name] = 0

    AUTO = {"vector": "ve", "scalar": "ae", "gpsimd": "ge"}

    def emit(self, eng, fn, waits=(), sig=None, n=1):
        waits = list(waits)
        if eng in self.AUTO:
            a = self.AUTO[eng]
            if self.val[a] > 0:
                waits.append((a, self.val[a]))
            if sig is None:
                sig = a
        wl0 = _tl(list(waits))
        tok = None
        wl = []
        for s, v in wl0:
            if s.startswith("D_"):
                v = self.val[s]
            if self.waited[eng].get(s, 0) >= v:
                continue
            self.waited[eng][s] = v
            wl.append((s, v))
        if sig is not None:
            self.val[sig] += n
            tok = (sig, self.val[sig])
        sems = self.sems

        def closure(e, wl=wl, fn=fn, tok=tok, n=n):
            for s, v in wl:
                e.wait_ge(sems[s], v)
            inst = fn(e)
            if tok is not None:
                inst.then_inc(sems[tok[0]], n)
        self.q[eng].append(closure)
        return tok

    def fence(self, *streams):
        return [("D_" + st, self.val["D_" + st]) for st in streams
                if ("D_" + st) in self.val]

    def dma(self, queue, stream, out, in_, waits=(), **kw):
        sig = "D_" + stream
        if sig not in self.sems:
            self.sems[sig] = self.ctx.enter_context(self.nc.semaphore(sig))
            self.val[sig] = 0
        return self.emit(queue, lambda e: e.dma_start(out=out, in_=in_, **kw),
                         waits=waits, sig=sig, n=16)


class Arena:

    def __init__(self, t):
        self.t = t
        self.n = t.shape[1]
        self.off = 0
        self.marks = []

    def mark(self):
        self.marks.append(self.off)

    def release(self):
        self.off = self.marks.pop()

    def f32(self, cols):
        cols = (cols + 1) // 2 * 2
        assert self.off + cols <= self.n, f"arena overflow {self.off}+{cols}>{self.n}"
        ap = self.t[:, self.off:self.off + cols]
        self.off += cols
        return ap

    def bf16(self, cols):
        return self.f32((cols + 1) // 2).bitcast(BF16)[:, 0:cols]


def bfree(ap, reps, inner):
    return ap.unsqueeze(2).broadcast_to([ap.shape[0], reps, inner])


def bmid(ap, mid):
    return ap.unsqueeze(1).broadcast_to([ap.shape[0], mid, ap.shape[1]])


def r3(ap, inner):
    return ap.rearrange("p (a b) -> p a b", b=inner)


def _finish(P, block, nc):
    @block.tensor
    def _(eng):
        for f in P.q["tensor"]:
            f(eng)

    @block.vector
    def _(eng):
        for f in P.q["vector"]:
            f(eng)

    @block.scalar
    def _(eng):
        for f in P.q["scalar"]:
            f(eng)

    @block.gpsimd
    def _(eng):
        for f in P.q["gpsimd"]:
            f(eng)

    @block.sync
    def _(eng):
        for f in P.q["sync"]:
            f(eng)
    return nc


def build_program(n_main=16, upto='p4'):
    nc = bass.Bass("TRN2", target_bir_lowering=False, debug=False)
    NZ = NPREF + n_main
    if upto == "p2pre":
        NZ_RUN = NPREF
    elif upto.startswith("p2v") or upto == "p2one":
        NZ_RUN = NPREF + 1
    else:
        NZ_RUN = NZ
    FLVL = int(upto[3]) if upto.startswith("p2v") else 9
    NT = NZ * CHUNK
    NM = n_main * CHUNK
    NR = NT + 3
    NTT = n_main
    NCOL = (NM + 511) // 512

    def ein(name, shape, dt=F32):
        return nc.dram_tensor(name, list(shape), dt, kind="ExternalInput").ap()

    xin = ein("xin", [NR, HID])
    wint = ein("wint", [NE, 128, KD, 128])
    wot = ein("wot", [32, 128, 64, 128])
    wbt = ein("wbt", [32, 128, 32, 128])
    convw = ein("convw", [128, NE_HBC * KCV])
    convb = ein("convb", [128, NE_HBC])
    nwc = ein("nwc", [128, INTER // 128])
    bbc = ein("bbc", [128, 32])
    dtbc = ein("dtbc", [128, 1])
    alogc = ein("alogc", [128, 1])
    drow = ein("drow", [1, 128])
    wrt = ein("wrt", [128, KD, 8])
    alt = ein("alt", [128, KD, ER])
    blt = ein("blt", [ER, HID])
    identf = ein("identf", [128, 128])
    identb = ein("identb", [128, 128], BF16)
    trimask = ein("trimask", [128, 128])
    onesrow = ein("onesrow", [1, 128])
    smask = ein("smask", [128, 1])

    out = nc.dram_tensor("out", [NM, HID], F32, kind="ExternalOutput").ap()

    def scrt(name, shape, dt):
        return nc.dram_tensor(name, list(shape), dt).ap()

    hsT_s = scrt("hsT_s", [NT, INTER], BF16)
    sg_s = scrt("sg_s", [NM, INTER], BF16)
    b_s = scrt("b_s", [NG * NS, NT], BF16)
    c_s = scrt("c_s", [NG * NS, NT], BF16)
    acum_s = scrt("acum_s", [128, NT], F32)
    y_s = scrt("y_s", [NM, INTER], BF16)
    g_s = scrt("g_s", [INTER, NM], BF16)
    x1t_s = scrt("x1t_s", [HID, NM], BF16)
    x1p_s = scrt("x1p_s", [NM, HID], F32)
    ffp_s = scrt("ffp_s", [NM, HID], F32)

    ctx = contextlib.ExitStack()
    with ctx:
        P = Prog(nc, ctx)

        def sb(name, shape, dt):
            return ctx.enter_context(nc.sbuf_tensor(name, list(shape), dt))

        idf = sb("idf", [128, 128], F32)
        idb = sb("idb", [128, 128], BF16)
        trim = sb("trim", [128, 128], F32)
        ones_f = sb("ones_f", [1, 128], F32)
        cwc = sb("cwc", [128, NE_HBC * KCV], F32)
        cbc = sb("cbc", [128, NE_HBC], F32)
        nwcs = sb("nwcs", [128, INTER // 128], F32)
        bbcs = sb("bbcs", [128, 32], F32)
        dtbs = sb("dtbs", [128, 1], F32)
        alogs = sb("alogs", [128, 1], F32)
        acol = sb("acol", [128, 1], F32)
        drows = sb("drows", [1, 128], F32)
        drep = sb("drep", [128, 128], BF16)
        smk = sb("smk", [128, 1], F32)
        wrts = sb("wrts", [128, KD * 8], BF16)
        dtT_all = sb("dtT_all", [128, NT], BF16)
        dsT_all = sb("dsT_all", [128, NT], BF16)
        nacT_all = sb("nacT_all", [128, NT], F32)
        arow_all = sb("arow_all", [1, NT], F32)
        mu1 = sb("mu1", [128, max(NTT, 1)], F32)
        nmr1 = sb("nmr1", [128, max(NTT, 1)], F32)
        rs1 = sb("rs1", [128, max(NTT, 1)], F32)
        ssq = sb("ssq", [128, 8], F32)
        exc = sb("exc", [128, 1], F32)
        vtmp = sb("vtmp", [128, 1], F32)
        rstd8 = sb("rstd8", [128, 8], F32)
        rstdb = sb("rstdb", [128, 8], BF16)
        combt = sb("combt", [128, 8], F32)
        zrow = sb("zrow", [128, 8], F32)
        qrow = sb("qrow", [128, 8], F32)
        m1c = sb("m1c", [128, 1], F32)
        m2c = sb("m2c", [128, 1], F32)
        denc = sb("denc", [128, 1], F32)
        gsb = [sb(f"gsb{i}", [128, 128], BF16) for i in range(2)]
        btk = [sb(f"btk{i}", [128, 128], BF16) for i in range(2)]
        amat = sb("amat", [128, 128], F32)
        afb = [sb(f"afb{i}", [1, 1024], F32) for i in range(2)]
        gstg = [sb(f"gstg{i}", [128, 128], BF16) for i in range(2)]
        xblk = [sb(f"xblk{i}", [128, 128], F32) for i in range(2)]
        x1stg = [sb(f"x1stg{i}", [128, 128], F32) for i in range(2)]
        hwstg = [sb(f"hwstg{i}", [128, 128], BF16) for i in range(2)]
        ffstg = [sb(f"ffstg{i}", [128, 128], F32) for i in range(2)]
        ystg = [sb(f"ystg{i}", [128, 512], BF16) for i in range(2)]
        gcol = [sb(f"gcol{i}", [128, 512], BF16) for i in range(4)]

        AR = Arena(sb("AR", [128, 39872], F32))

        ps = ctx.enter_context(nc.psum_tensor("ps", [128, 4096], F32))
        psb = ps[:, :].bitcast(BF16)

        def bank(i, w=512, off=0):
            return ps[:, i * 512 + off: i * 512 + off + w]

        def bwin(i, w=128, off=0):
            return psb[:, i * 1024 + off: i * 1024 + off + w]

        block = ctx.enter_context(nc.Block())

        def MM(o, a, b, s1, s2, waits=(), sig="pe"):
            return P.emit("tensor",
                          lambda e, o=o, a=a, b=b, s1=s1, s2=s2:
                          e.matmul(o, a, b, start=s1, stop=s2),
                          waits=waits, sig=sig)

        def TR(o, i_, idn, waits=(), sig="pe"):
            return P.emit("tensor",
                          lambda e, o=o, i_=i_, idn=idn: e.transpose(o, i_, idn),
                          waits=waits, sig=sig)

        def VCOPY(o, i_, waits=(), sig="ve", eng="vector"):
            sg = {"vector": "ve", "gpsimd": "ge"}[eng]
            return P.emit(eng, lambda e, o=o, i_=i_: e.tensor_copy(o, i_),
                          waits=waits, sig=sg if sig else None)

        def VTT(o, a, b, op, waits=(), sig="ve", eng="vector"):
            sg = {"vector": "ve", "gpsimd": "ge"}[eng] if sig else None
            return P.emit(eng, lambda e, o=o, a=a, b=b, op=op:
                          e.tensor_tensor(o, a, b, op), waits=waits, sig=sg)

        def VTS(o, a, s1_, s2_, op0, op1=None, waits=(), sig="ve", accum=None):
            if op1 is None:
                return P.emit("vector",
                              lambda e, o=o, a=a, s1_=s1_, op0=op0:
                              e.tensor_scalar(o, a, s1_, None, op0),
                              waits=waits, sig=sig)
            return P.emit("vector",
                          lambda e, o=o, a=a, s1_=s1_, s2_=s2_, op0=op0, op1=op1,
                          accum=accum:
                          e.tensor_scalar(o, a, s1_, s2_, op0, op1,
                                          accum_out=accum),
                          waits=waits, sig=sig)

        def ACT(o, i_, fn, bias=0.0, scale=1.0, waits=(), sig="ae", accum=None):
            return P.emit("scalar",
                          lambda e, o=o, i_=i_, fn=fn, bias=bias, scale=scale,
                          accum=accum:
                          e.activation(o, i_, fn, bias=bias, scale=scale,
                                       accum_out=accum),
                          waits=waits, sig=sig)

        t_loads = []
        for dst, src in ((idf[:, :], identf), (idb[:, :], identb),
                         (trim[:, :], trimask), (ones_f[:, :], onesrow),
                         (cbc[:, :], convb), (cwc[:, :], convw),
                         (nwcs[:, :], nwc), (bbcs[:, :], bbc),
                         (dtbs[:, :], dtbc), (alogs[:, :], alogc),
                         (drows[:, :], drow), (smk[:, :], smask)):
            t_loads.append(P.dma("sync", "ini", dst, src))
        t_loads.append(P.dma("gpsimd", "iniw",
                             wrts[:, :].rearrange("p (k e) -> p k e", e=8), wrt))
        t0 = ACT(acol[:, 0:1], alogs[:, 0:1], AF.Exp, waits=t_loads)
        tk_acol = VTS(acol[:, 0:1], acol[:, 0:1], -1.0, None, ALU.mult, waits=[t0])
        t0 = MM(bank(7, 128), ones_f[0:1, :], drows[0:1, :], True, True,
                waits=t_loads)
        tk_drep = VCOPY(drep[:, :], bank(7, 128), waits=[t0])

        AR.mark()
        XTW = 1284
        xt = r3(AR.bf16(KD * XTW), XTW)
        xtile = [AR.bf16(HID)]
        tstg = [AR.bf16(1280) for _ in range(2)]
        wst = [r3(AR.bf16(KD * 128), 128) for _ in range(2)]
        cscr = [AR.f32(1284) for _ in range(2)]
        xsil = [AR.bf16(1284) for _ in range(2)]
        spl = AR.f32(1284)
        adt = AR.f32(1284)
        dtv = AR.f32(NT)
        acum = AR.f32(NT)

        if NR <= 1283:
            passes = [(0, NR, 0)]
        else:
            passes = [(0, 1283, 0), (1280, NR - 1280, 1280)]
        HBC0, HBC1 = NE_GATE, NE_GATE + NE_HBC

        last_wst = [None, None]
        last_epil = [None, None]
        last_xsil_u = [None, None]
        last_cscr_u = [None, None]
        last_tstg_u = [None, None]
        last_xtile_u = [None, None]
        tk_b7 = tk_drep
        sg_toks, hs_toks, bc_toks, dt_prep_toks = [], [], [], []
        mm_tok = None
        last_spl_u = None

        for (r0, W, q0) in passes:
            ntile = (W + 127) // 128
            ev_last = None
            for it in range(ntile):
                rlo = r0 + it * 128
                rows = min(128, r0 + W - rlo)
                xb = 0
                t_x = P.dma("gpsimd", f"xld{xb}", xtile[xb][0:rows, :], xin[rlo:rlo + rows, :],
                            waits=[last_xtile_u[xb], mm_tok])
                for k in range(KD):
                    slot = k % 2
                    o_ps = bwin(6 + slot, 128)
                    t_t = TR(o_ps[0:128, 0:rows],
                             xtile[xb][0:rows, k * 128:(k + 1) * 128],
                             idb[0:rows, 0:rows],
                             waits=[t_x, last_tstg_u[slot], tk_b7])
                    ev_last = VCOPY(xt[:, k, it * 128: it * 128 + rows],
                                    o_ps[0:128, 0:rows], waits=[t_t])
                    last_tstg_u[slot] = ev_last
                last_xtile_u[xb] = ev_last
            xt_ready = ev_last
            tk_b7 = ev_last

            cols = []
            off = 0
            while off < W:
                cw = min(512, W - off)
                cols.append((off, cw))
                off += cw

            for ei in range(NE):
                is_gate = ei < HBC0
                is_hbc = HBC0 <= ei < HBC1
                is_dt = ei == NE - 1
                glo = PAD - r0 if r0 < PAD else 3
                if is_gate and W - glo <= 0:
                    continue
                buf = ei % 2
                pb = 0 if buf == 0 else 3
                t_w = P.dma("gpsimd", f"wst{buf}", wst[buf][:, :, :], wint[ei, :, :, :],
                            waits=[last_wst[buf]])
                mm_tok = None
                for ci, (coff, cw) in enumerate(cols):
                    for k in range(KD):
                        mm_tok = MM(bank(pb + ci, cw), wst[buf][:, k, :],
                                    xt[:, k, coff:coff + cw],
                                    (k == 0), (k == KD - 1),
                                    waits=[t_w, xt_ready,
                                           last_epil[buf] if (k == 0 and ci == 0)
                                           else None],
                                    sig="pe" if k == KD - 1 else None)
                last_wst[buf] = mm_tok

                def pap(lo, w, pb=pb):
                    return ps[:, pb * 512 + lo: pb * 512 + lo + w]

                if is_gate:
                    gw = W - glo
                    m0 = (r0 + glo) - PAD
                    t_s = ACT(xsil[buf][:, 0:gw], pap(glo, gw), AF.Silu,
                              waits=[mm_tok, last_xsil_u[buf]])
                    last_epil[buf] = t_s
                    nsub = gw // 128
                    ev = None
                    for sti in range(nsub):
                        slot = sti % 2
                        o_ps = bwin(6 + slot, 128)
                        t_t = TR(o_ps, xsil[buf][:, sti * 128:(sti + 1) * 128],
                                 idb[:, :],
                                 waits=[t_s, last_tstg_u[buf] if sti < 2 else None,
                                        tk_b7])
                        ev = VCOPY(tstg[buf][:, sti * 128:(sti + 1) * 128], o_ps,
                                   waits=[t_t])
                        tk_b7 = ev
                    dtok = P.dma("sync", f"tt{buf}",
                                 sg_s[m0: m0 + gw, ei * 128:(ei + 1) * 128]
                                 .rearrange("(s p) e -> p s e", p=128),
                                 r3(tstg[buf][:, 0:gw], 128), waits=[ev])
                    last_tstg_u[buf] = dtok
                    sg_toks.append(dtok)
                    last_xsil_u[buf] = dtok
                elif is_hbc:
                    hi = ei - HBC0
                    nq_t = W - 3
                    cp = VCOPY(cscr[buf][:, 0:W], pap(0, W),
                               waits=[mm_tok, last_cscr_u[buf]])
                    last_epil[buf] = cp
                    mac = P.emit("vector",
                                 lambda e, buf=buf, nq_t=nq_t, hi=hi:
                                 e.tensor_scalar(spl[:, 0:nq_t],
                                                 cscr[buf][:, 0:nq_t],
                                                 cwc[:, hi * 4: hi * 4 + 1], None,
                                                 ALU.mult),
                                 waits=[cp, last_spl_u])
                    for kk in (1, 2, 3):
                        mac = P.emit("vector",
                                     lambda e, buf=buf, nq_t=nq_t, hi=hi, kk=kk:
                                     e.scalar_tensor_tensor(
                                         spl[:, 0:nq_t],
                                         cscr[buf][:, kk:kk + nq_t],
                                         cwc[:, hi * 4 + kk: hi * 4 + kk + 1],
                                         spl[:, 0:nq_t], ALU.mult, ALU.add),
                                     sig="ve" if kk == 3 else None)
                    last_cscr_u[buf] = mac
                    t_s = ACT(xsil[buf][:, 0:nq_t], spl[:, 0:nq_t], AF.Silu,
                              bias=cbc[:, hi:hi + 1],
                              waits=[mac, last_xsil_u[buf]])
                    last_spl_u = t_s
                    if hi < 64:
                        nsub = nq_t // 128
                        ev = None
                        for sti in range(nsub):
                            slot = sti % 2
                            o_ps = bwin(6 + slot, 128)
                            t_t = TR(o_ps,
                                     xsil[buf][:, sti * 128:(sti + 1) * 128],
                                     idb[:, :],
                                     waits=[t_s,
                                            last_tstg_u[buf] if sti < 2 else None,
                                            tk_b7])
                            ev = VCOPY(tstg[buf][:, sti * 128:(sti + 1) * 128],
                                       o_ps, waits=[t_t])
                            tk_b7 = ev
                        dtok = P.dma("sync", f"tt{buf}",
                                     hsT_s[q0: q0 + nq_t,
                                           hi * 128:(hi + 1) * 128]
                                     .rearrange("(s p) e -> p s e", p=128),
                                     r3(tstg[buf][:, 0:nq_t], 128), waits=[ev])
                        last_tstg_u[buf] = dtok
                        hs_toks.append(dtok)
                        last_xsil_u[buf] = dtok
                    else:
                        tgt = b_s if hi < 72 else c_s
                        row0 = (hi - 64) * 128 if hi < 72 else (hi - 72) * 128
                        dtok = P.dma("sync", f"bcw{buf}", tgt[row0:row0 + 128, q0:q0 + nq_t],
                                     xsil[buf][:, 0:nq_t], waits=[t_s])
                        bc_toks.append(dtok)
                        last_xsil_u[buf] = dtok
                elif is_dt:
                    nq_t = W - 3
                    t1 = VTS(spl[:, 0:nq_t], pap(3, nq_t), dtbs[:, 0:1], None,
                             ALU.add, waits=[mm_tok, last_cscr_u[buf],
                                             last_spl_u])
                    last_epil[buf] = t1
                    ACT(adt[:, 0:nq_t], spl[:, 0:nq_t], AF.Abs, waits=[t1],
                        sig=None)
                    ACT(adt[:, 0:nq_t], adt[:, 0:nq_t], AF.Exp, scale=-1.0,
                        sig=None)
                    t2 = ACT(adt[:, 0:nq_t], adt[:, 0:nq_t], AF.Ln, bias=1.0)
                    P.emit("vector", lambda e, nq_t=nq_t:
                           e.tensor_relu(spl[:, 0:nq_t], spl[:, 0:nq_t]),
                           waits=[t2], sig=None)
                    VTT(spl[:, 0:nq_t], spl[:, 0:nq_t], adt[:, 0:nq_t], ALU.add,
                        sig=None)
                    VTS(dtv[:, q0:q0 + nq_t], spl[:, 0:nq_t], 100.0, None,
                        ALU.min)
                    t4 = VTS(adt[:, 0:nq_t], dtv[:, q0:q0 + nq_t], acol[:, 0:1],
                             None, ALU.mult, waits=[tk_acol])
                    nch = nq_t // 128
                    for z_ in range(nch):
                        t4 = P.emit("vector",
                                    lambda e, z_=z_, q0=q0: e.tensor_tensor_scan(
                                        acum[:, q0 + z_ * 128:
                                             q0 + (z_ + 1) * 128],
                                        adt[:, z_ * 128:(z_ + 1) * 128],
                                        adt[:, z_ * 128:(z_ + 1) * 128],
                                        0.0, ALU.add, ALU.bypass),
                                    sig="ve")
                    dt_done = t4
                    dt_prep_toks.append(
                        P.dma("sync", "acw", acum_s[:, q0:q0 + nq_t],
                              acum[:, q0:q0 + nq_t], waits=[dt_done]))
                    for z_ in range(nch):
                        qz = q0 + z_ * 128
                        t5 = ACT(spl[:, 0:128], acum[:, qz:qz + 128], AF.Exp,
                                 bias=acum[:, qz + 127:qz + 128], scale=-1.0,
                                 waits=[dt_done, last_spl_u])
                        t6 = TR(bank(7, 128), spl[:, 0:128], idf[:, :],
                                waits=[t5, tk_b7])
                        last_spl_u = t6
                        t6 = VCOPY(dsT_all[:, qz:qz + 128], bank(7, 128),
                                   waits=[t6])
                        t7 = TR(bank(7, 128), dtv[:, qz:qz + 128], idf[:, :],
                                waits=[t6])
                        t7 = VCOPY(dtT_all[:, qz:qz + 128], bank(7, 128),
                                   waits=[t7])
                        t8 = TR(bank(7, 128), acum[:, qz:qz + 128], idf[:, :],
                                waits=[t7])
                        t8 = VTS(nacT_all[:, qz:qz + 128], bank(7, 128), -1.0,
                                 None, ALU.mult, waits=[t8])
                        t9 = TR(bank(7, 128)[0:1, 0:128],
                                acum[:, qz + 127:qz + 128],
                                idf[:, :], waits=[t8])
                        t9 = ACT(arow_all[0:1, qz:qz + 128],
                                 bank(7, 128)[0:1, 0:128],
                                 AF.Exp, waits=[t9])
                        tk_b7 = t9
                        dt_prep_toks.append(t9)

        p1_done = [mm_tok, last_epil[0], last_epil[1], last_xsil_u[0],
                   last_xsil_u[1], last_cscr_u[0], last_cscr_u[1],
                   last_tstg_u[0], last_tstg_u[1], tk_b7] + dt_prep_toks
        AR.release()
        if upto == "p1":
            P.emit("sync", lambda e: None,
                   waits=p1_done + P.fence(*[k[2:] for k in P.val if k.startswith("D_")]))
            return _finish(P, block, nc)

        AR.mark()
        prevT = AR.f32(INTER)
        prevB = AR.bf16(INTER)
        hst = [AR.bf16(INTER) for _ in range(2)]
        bcB = [AR.bf16(NG * 128) for _ in range(2)]
        bcC = [AR.bf16(NG * 128) for _ in range(2)]
        dmw = [AR.f32(512) for _ in range(4)]
        dmv = [AR.f32(512) for _ in range(4)]
        ltb = [AR.bf16(512) for _ in range(4)]
        ea2 = [AR.bf16(512) for _ in range(4)]
        mtb = [AR.bf16(512) for _ in range(4)]
        cet = [AR.bf16(512) for _ in range(4)]
        xchb = [AR.bf16(256) for _ in range(4)]
        xchd = [AR.bf16(256) for _ in range(4)]
        xchD = [AR.bf16(256) for _ in range(4)]
        prec = [AR.f32(512) for _ in range(2)]

        NB = 32
        tk_zero = P.emit("vector", lambda e: e.memzero(prevT[:, :]),
                         waits=p1_done, sig="ve")
        last_hst_u = [p1_done, p1_done]
        last_bc_u = [p1_done, p1_done]
        last_rb = [p1_done, p1_done, p1_done]
        last_y = [p1_done, p1_done]
        last_s = [p1_done, p1_done]
        last_dm = [p1_done] * 4
        last_lt = [None] * 4
        last_mt = [None] * 4
        last_ct = [None] * 4
        last_xc = [tk_zero] * 4
        last_af = [None, None]
        last_gb = [None, None]
        last_ys = [None, None]
        last_prec = [None, None]
        y_toks = [None] * max(n_main, 1)
        tk_prev_chain = tk_zero
        cur_af = None
        tk_gsb = None

        for zz in range(NZ_RUN):
            is_main = zz >= NPREF
            z = zz - NPREF
            hb2 = zz % 2
            t_hst = P.dma("sync", f"hst{hb2}", hst[hb2][:, :],
                          hsT_s[zz * 128:(zz + 1) * 128, :],
                          waits=[last_hst_u[hb2]] + (hs_toks if zz <= NPREF else []))
            t_bcb, t_bcc = [], []
            for g in range(NG):
                t_bcb.append(P.dma("scalar", f"bc{hb2}", bcB[hb2][:, g * 128:(g + 1) * 128],
                                   b_s[g * 128:(g + 1) * 128,
                                       zz * 128:(zz + 1) * 128],
                                   waits=[last_bc_u[hb2]] + (bc_toks if zz <= NPREF else [])))
                if is_main:
                    t_bcc.append(P.dma("scalar", f"bc{hb2}",
                                       bcC[hb2][:, g * 128:(g + 1) * 128],
                                       c_s[g * 128:(g + 1) * 128,
                                           zz * 128:(zz + 1) * 128],
                                       waits=[last_bc_u[hb2]] + (bc_toks if zz <= NPREF else [])))
            t_am = MM(bank(7, 128), ones_f[0:1, :],
                      arow_all[0:1, zz * 128: zz * 128 + 128], True, True,
                      waits=[tk_b7] + dt_prep_toks)
            t_am = VCOPY(amat[:, :], bank(7, 128), waits=[t_am])
            tk_b7 = t_am
            tk_prevB = VCOPY(prevB[:, :], prevT[:, :], waits=[tk_prev_chain])

            for hb in range(NB):
                g = hb // 4
                i4 = hb % 4
                rot2 = hb % 4
                rot3 = hb % 3
                rot4 = hb % 4
                ybk = (hb // 2) % 2
                sbk = 2 + (hb // 2) % 2
                h0 = hb * 4
                gpar = g % 2

                if i4 == 0:
                    if is_main:
                        t_g = MM(bank(7, 128), bcB[hb2][:, g * 128:(g + 1) * 128],
                                 bcC[hb2][:, g * 128:(g + 1) * 128], True, True,
                                 waits=[tk_b7, t_bcb[g], t_bcc[g]])
                        t_bt = TR(bwin(7, 128, 256),
                                  bcB[hb2][:, g * 128:(g + 1) * 128],
                                  idb[:, :], waits=[])
                        t_g2 = VCOPY(gsb[gpar][:, :], bank(7, 128),
                                     waits=[t_bt, last_gb[gpar]])
                        t_b2 = VCOPY(btk[gpar][:, :], bwin(7, 128, 256),
                                     waits=[])
                        tk_b7 = t_b2
                        tk_gsb = t_g2
                        tk_btk = t_b2
                    else:
                        t_bt = TR(bwin(7, 128, 256),
                                  bcB[hb2][:, g * 128:(g + 1) * 128],
                                  idb[:, :], waits=[tk_b7, t_bcb[g]])
                        t_b2 = VCOPY(btk[gpar][:, :], bwin(7, 128, 256),
                                     waits=[t_bt, last_gb[gpar]])
                        tk_b7 = t_b2
                        tk_btk = t_b2

                hsl = hst[hb2][:, h0 * 64:(h0 + 4) * 64]
                t_xc = VTT(r3(xchb[rot4][:, :], 64), r3(hsl, 64),
                           bfree(dtT_all[:, zz * 128 + h0: zz * 128 + h0 + 4],
                                 4, 64),
                           ALU.mult, waits=[t_hst, last_xc[rot4]], sig="ve",
                           eng="gpsimd")
                t_xd = VTT(r3(xchd[rot4][:, :], 64), r3(xchb[rot4][:, :], 64),
                           bfree(dsT_all[:, zz * 128 + h0: zz * 128 + h0 + 4],
                                 4, 64),
                           ALU.mult, sig="ve", eng="gpsimd")
                if is_main:
                    t_xD = VTT(r3(xchD[rot4][:, :], 64), r3(hsl, 64),
                               bfree(drep[:, h0:h0 + 4], 4, 64),
                               ALU.mult, waits=[tk_drep], sig="ve",
                               eng="gpsimd")
                else:
                    t_xD = t_xd

                if is_main and FLVL >= 2:
                    if hb % 2 == 0:
                        apar = (hb // 2) % 2
                        cur_af = P.dma("scalar", f"af{apar}",
                                       afb[apar][0:1, :].rearrange(
                                           "p (h l) -> p h l", l=128),
                                       acum_s[h0:h0 + 8,
                                              zz * 128:(zz + 1) * 128].unsqueeze(0),
                                       waits=[last_af[apar]] + dt_prep_toks)
                        last_af[apar] = cur_af
                    rbk = 4 + rot3
                    t_rb = MM(bank(rbk), ones_f[0:1, :],
                              afb[(hb // 2) % 2][0:1,
                                                 (hb % 2) * 512:(hb % 2) * 512 + 512],
                              True, True, waits=[cur_af, last_rb[rot3]])
                    if FLVL < 3:
                        continue
                    t_da = VTT(r3(dmw[rot2][:, :], 128), r3(bank(rbk), 128),
                               bmid(trim[:, :], 4), ALU.add,
                               waits=[t_rb, last_dm[rot2]])
                    t_db = VTT(r3(dmv[rot2][:, :], 128), r3(dmw[rot2][:, :], 128),
                               bfree(nacT_all[:, zz * 128 + h0:
                                              zz * 128 + h0 + 4], 4, 128),
                               ALU.add, waits=[last_lt[rot2]])
                    t_ea = ACT(ea2[rot2][:, :], bank(rbk), AF.Exp,
                               waits=[t_rb, last_ct[rot2], t_da])
                    last_rb[rot3] = [t_da, t_ea]
                    t_lt = ACT(ltb[rot2][:, :], dmv[rot2][:, :], AF.Exp,
                               waits=[t_db, last_mt[rot2]])
                    last_dm[rot2] = t_lt
                    if FLVL < 4:
                        continue
                    t_mt = VTT(r3(mtb[rot2][:, :], 128), r3(ltb[rot2][:, :], 128),
                               bmid(gsb[gpar][:, :], 4), ALU.mult,
                               waits=[t_lt, tk_gsb])
                    last_lt[rot2] = t_mt
                    t_ct = VTT(r3(cet[rot2][:, :], 128),
                               bmid(bcC[hb2][:, g * 128:(g + 1) * 128], 4),
                               r3(ea2[rot2][:, :], 128), ALU.mult,
                               waits=[t_ea, t_bcc[g]])
                    last_ct[rot2] = t_ct
                    t_y = None
                    if FLVL < 5:
                        continue
                    for i in range(4):
                        h = h0 + i
                        o_y = bank(ybk)[:, (h % 8) * 64:(h % 8) * 64 + 64]
                        MM(o_y, mtb[rot2][:, i * 128:(i + 1) * 128],
                           xchb[rot4][:, i * 64:(i + 1) * 64], True, False,
                           waits=[t_mt, t_xc,
                                  last_y[ybk] if (hb % 2 == 0 and i == 0)
                                  else None],
                           sig=None)
                        MM(o_y, cet[rot2][:, i * 128:(i + 1) * 128],
                           prevB[:, h * 64:(h + 1) * 64], False, False,
                           waits=[t_ct, tk_prevB], sig=None)
                        t_y = MM(o_y, idb[:, :],
                                 xchD[rot4][:, i * 64:(i + 1) * 64],
                                 False, True, waits=[t_xD],
                                 sig="pe" if i == 3 else None)
                    last_mt[rot2] = t_y
                    last_ct[rot2] = [t_ct, t_y]
                else:
                    t_y = None

                t_st = None
                for i in range(4):
                    h = h0 + i
                    o_s = bank(sbk)[:, (h % 8) * 64:(h % 8) * 64 + 64]
                    t_st = MM(o_s, btk[gpar][:, :],
                              xchd[rot4][:, i * 64:(i + 1) * 64],
                              True, True,
                              waits=[tk_btk, t_xd,
                                     last_s[sbk - 2] if (hb % 2 == 0 and i == 0)
                                     else None],
                              sig="pe" if i == 3 else None)
                last_gb[gpar] = t_y if is_main else t_st
                last_xc[rot4] = t_st

                if hb % 2 == 1:
                    h8 = (hb // 2) * 8
                    ypar = (hb // 2) % 2
                    if is_main and FLVL >= 6:
                        t_ye = VCOPY(ystg[ypar][:, :], bank(ybk),
                                     waits=[t_y, last_ys[ypar]])
                        last_y[ybk] = t_ye
                        t_yd = P.dma("sync", f"ys{ypar}",
                                     y_s[z * 128:(z + 1) * 128,
                                         h8 * 64: h8 * 64 + 512],
                                     ystg[ypar][:, :], waits=[t_ye])
                        last_ys[ypar] = t_yd
                        if hb == NB - 1:
                            y_toks[z] = t_yd
                    ppar = (hb // 2) % 2
                    t_p1 = VTT(r3(prec[ppar][:, :], 64),
                               r3(prevT[:, h8 * 64: h8 * 64 + 512], 64),
                               bfree(amat[:, h8:h8 + 8], 8, 64), ALU.mult,
                               waits=[t_am, tk_prevB, last_prec[ppar]])
                    t_p2 = VTT(prevT[:, h8 * 64: h8 * 64 + 512],
                               prec[ppar][:, :],
                               bank(sbk), ALU.add, waits=[t_st, t_p1])
                    last_prec[ppar] = t_p2
                    last_s[sbk - 2] = t_p2
                    tk_prev_chain = t_p2

            last_hst_u[hb2] = last_xc
            last_bc_u[hb2] = tk_prev_chain
            if zz == NPREF - 1:
                tk_prev_chain = VTS(prevT[:, :], prevT[:, :], smk[:, 0:1], None,
                                    ALU.mult, waits=[tk_prev_chain])

        p2_done = [tk_prev_chain, tk_b7, tk_prevB] + last_hst_u[0] + \
            last_hst_u[1] + _tl(last_ys) + _tl(last_y) + _tl(last_s) + \
            _tl(last_mt) + _tl(last_ct) + _tl(last_dm) + _tl(last_prec)
        AR.release()
        if upto in ("p2", "p2pre", "p2one"):
            P.emit("sync", lambda e: None,
                   waits=p2_done + P.fence(*[k[2:] for k in P.val if k.startswith("D_")]))
            return _finish(P, block, nc)

        AR.mark()
        ybuf = AR.bf16(INTER)
        sgbuf = AR.bf16(INTER)
        g_toks = []
        tk_gst = [p2_done, p2_done]
        t_gn_last = p2_done
        for z in range(n_main):
            t_y = P.dma("sync", "yr", ybuf[:, :], y_s[z * 128:(z + 1) * 128, :],
                        waits=P.fence("ys0", "ys1") + [t_gn_last])
            t_sg = P.dma("sync", "sgr", sgbuf[:, :], sg_s[z * 128:(z + 1) * 128, :],
                         waits=(P.fence("tt0", "tt1") if z == 0 else []) + [t_gn_last])
            t_g = VTT(ybuf[:, :], ybuf[:, :], sgbuf[:, :], ALU.mult,
                      waits=[t_y, t_sg])
            t_sq = None
            for grp in range(8):
                t_sq = ACT(sgbuf[:, 0:2048].bitcast(F32)[:, 0:1024],
                           ybuf[:, grp * 1024:(grp + 1) * 1024],
                           AF.Square, waits=[t_g], accum=ssq[:, grp:grp + 1])
            t_v = VTS(rstd8[:, :], ssq[:, :], 1.0 / 1024.0, EPS, ALU.mult,
                      ALU.add, waits=[t_sq])
            t_v = ACT(rstd8[:, :], rstd8[:, :], AF.Ln, waits=[t_v])
            t_v = ACT(rstd8[:, :], rstd8[:, :], AF.Exp, scale=-0.5)
            t_v = VCOPY(rstdb[:, :], rstd8[:, :], waits=[t_v])
            t_gn = VTT(r3(ybuf[:, :], 1024), r3(ybuf[:, :], 1024),
                       bfree(rstdb[:, :], 8, 1024), ALU.mult, waits=[t_v])
            t_gn_last = t_gn
            dtok = None
            for ct in range(64):
                slot = ct % 2
                o_ps = bwin(slot, 128)
                t_t = TR(o_ps, ybuf[:, ct * 128:(ct + 1) * 128], idb[:, :],
                         waits=[t_gn, tk_gst[slot]])
                ev = P.emit("vector",
                            lambda e, slot=slot, ct=ct, o_ps=o_ps:
                            e.tensor_scalar(gstg[slot][:, :], o_ps,
                                            nwcs[:, ct:ct + 1], None, ALU.mult),
                            waits=[t_t], sig="ve")
                dtok = P.dma("sync", f"gw{slot}",
                             g_s[ct * 128:(ct + 1) * 128, z * 128:(z + 1) * 128],
                             gstg[slot][:, :], waits=[ev])
                tk_gst[slot] = dtok
            g_toks.append(dtok)
        p25_done = _tl(tk_gst) + [t_gn_last]
        AR.release()
        if upto == "p25":
            P.emit("sync", lambda e: None,
                   waits=p25_done + P.fence(*[k[2:] for k in P.val if k.startswith("D_")]))
            return _finish(P, block, nc)

        AR.mark()
        x1ct = r3(AR.bf16(KD * 128), 128)
        x1pt = AR.f32(HID)
        x1bt = AR.bf16(HID)
        stat_sc = AR.f32(HID)
        aos = AR.f32(4 * 512)
        outt = AR.f32(HID)
        hwT = AR.bf16(NM)
        wof = AR.bf16(4 * 64 * 128)
        wob = [r3(wof[:, j * 64 * 128:(j + 1) * 64 * 128], 128) for j in range(4)]
        wbb = [r3(wof[:, j * 32 * 128:(j + 1) * 32 * 128]
                  if False else wof[:, (4 * 64 * 128 // 2) + j * 32 * 128:
                                    (4 * 64 * 128 // 2) + (j + 1) * 32 * 128], 128)
               for j in range(4)]

        last_wob = [p25_done] * 4
        last_gcol = [p25_done] * 4
        last_aos = [p25_done] * 4
        last_xb = [None, None]
        last_x1s = [None, None]
        bank_free = [p25_done] * 8
        x1p_toks = []
        for ob in range(8):
            t_w = []
            for j in range(4):
                t_w.append(P.dma("gpsimd", "wobl", wob[j][:, :, :], wot[ob * 4 + j],
                                 waits=[last_wob[j]]))
            mm_last = [None] * 4
            for ci in range(NCOL):
                cw = min(512, NM - ci * 512)
                cpar = ci % 2
                for k in range(64):
                    gp = k % 4
                    t_gc = P.dma("sync" if gp < 2 else "scalar", f"gc{gp}",
                                 gcol[gp][:, 0:cw],
                                 g_s[k * 128:(k + 1) * 128,
                                     ci * 512: ci * 512 + cw],
                                 waits=[last_gcol[gp]] +
                                 (P.fence("gw0", "gw1")
                                  if (ob == 0 and ci == 0 and k < 4) else []))
                    for ot in range(4):
                        bk = ot + 4 * cpar
                        mm_last[ot] = MM(bank(bk, cw), wob[ot][:, k, :],
                                         gcol[gp][:, 0:cw], (k == 0), (k == 63),
                                         waits=[t_gc, t_w[ot],
                                                bank_free[bk] if k == 0
                                                else None],
                                         sig="pe" if ot == 3 else None)
                    last_gcol[gp] = mm_last[3]
                for ot in range(3):
                    mm_last[ot] = mm_last[3]
                for ot in range(4):
                    bk = ot + 4 * cpar
                    oidx = ob * 4 + ot
                    t_e = ACT(aos[:, ot * 512: ot * 512 + cw], bank(bk, cw),
                              AF.Copy, waits=[mm_last[ot], last_aos[ot]])
                    nsub = cw // 128
                    t_tr = None
                    for sub in range(nsub):
                        t_tr = TR(bank(bk, 128, sub * 128),
                                  aos[:, ot * 512 + sub * 128:
                                      ot * 512 + sub * 128 + 128],
                                  idf[:, :], waits=[t_e],
                                  sig="pe" if sub == nsub - 1 else None)
                    last_aos[ot] = t_tr
                    dlast = None
                    for sub in range(nsub):
                        xpar = sub % 2
                        mt0 = ci * 512 + sub * 128
                        t_xb = P.dma("scalar", f"xb{xpar}", xblk[xpar][:, :],
                                     xin[PAD + mt0: PAD + mt0 + 128,
                                         oidx * 128:(oidx + 1) * 128],
                                     waits=[last_xb[xpar]])
                        t_ad = VTT(x1stg[xpar][:, :], bank(bk, 128, sub * 128),
                                   xblk[xpar][:, :], ALU.add,
                                   waits=[t_tr, t_xb, last_x1s[xpar]])
                        last_xb[xpar] = t_ad
                        dlast = P.dma("sync", f"x1w{xpar}",
                                      x1p_s[mt0:mt0 + 128,
                                            oidx * 128:(oidx + 1) * 128],
                                      x1stg[xpar][:, :], waits=[t_ad])
                        last_x1s[xpar] = dlast
                    bank_free[bk] = dlast
            for j in range(4):
                last_wob[j] = mm_last[j]
            x1p_toks.append(dlast)

        if upto == "p3":
            P.emit("sync", lambda e: None,
                   waits=_tl(x1p_toks) + P.fence(*[k[2:] for k in P.val if k.startswith("D_")]))
            AR.release()
            return _finish(P, block, nc)
        last_x1T = x1p_toks + _tl(tk_gst[0]) + _tl(tk_gst[1])
        x1T_toks = []
        tk_x1pt = x1p_toks
        tk_gstg = [last_x1T, last_x1T]
        for i in range(NTT):
            t_l = P.dma("sync", "x1r", x1pt[:, :], x1p_s[i * 128:(i + 1) * 128, :],
                        waits=(P.fence("x1w0", "x1w1") if i == 0 else []) +
                        _tl(tk_x1pt))
            t_mu = VTS(stat_sc[:, :], x1pt[:, :], 1.0 / HID, 0.0, ALU.mult,
                       ALU.add, waits=[t_l], accum=mu1[:, i:i + 1])
            t_s2 = ACT(stat_sc[:, :], x1pt[:, :], AF.Square, scale=1.0 / 64.0,
                       waits=[t_mu], accum=exc[:, 0:1])
            t_v = VTT(vtmp[:, 0:1], mu1[:, i:i + 1], mu1[:, i:i + 1], ALU.mult,
                      waits=[t_mu])
            t_v = VTT(vtmp[:, 0:1], exc[:, 0:1], vtmp[:, 0:1], ALU.subtract,
                      waits=[t_s2])
            t_v = VTS(vtmp[:, 0:1], vtmp[:, 0:1], EPS, None, ALU.add)
            t_v = ACT(vtmp[:, 0:1], vtmp[:, 0:1], AF.Ln, waits=[t_v])
            t_v = ACT(rs1[:, i:i + 1], vtmp[:, 0:1], AF.Exp, scale=-0.5)
            t_n = VTT(nmr1[:, i:i + 1], mu1[:, i:i + 1], rs1[:, i:i + 1],
                      ALU.mult, waits=[t_v])
            t_n = VTS(nmr1[:, i:i + 1], nmr1[:, i:i + 1], -1.0, None, ALU.mult)
            t_x1 = P.emit("vector",
                          lambda e, i=i: e.tensor_scalar(
                              x1bt[:, :], x1pt[:, :], rs1[:, i:i + 1],
                              nmr1[:, i:i + 1], ALU.mult, ALU.add),
                          waits=[t_n], sig="ve")
            tk_x1pt = t_x1
            ev = None
            for k in range(KD):
                slot = k % 2
                o_ps = bwin(slot, 128)
                t_t = TR(o_ps, x1bt[:, k * 128:(k + 1) * 128], idb[:, :],
                         waits=[t_x1, tk_gstg[slot]])
                ev = VCOPY(gstg[slot][:, :], o_ps, waits=[t_t])
                ev = P.dma("sync", f"gw{slot}", x1t_s[k * 128:(k + 1) * 128,
                                         i * 128:(i + 1) * 128],
                           gstg[slot][:, :], waits=[ev])
                tk_gstg[slot] = ev
            x1T_toks.append(ev)

        alts = stat_sc[:, 0:2048].bitcast(BF16)
        blts = stat_sc[:, 2048:4096].bitcast(BF16)
        stat_gate = [("ae", P.val["ae"]), ("ve", P.val["ve"])]
        t_alt = P.dma("gpsimd", "iniw", alts.rearrange("p (k e) -> p k e", e=ER),
                      alt, waits=stat_gate)
        t_blt = P.dma("gpsimd", "iniw", blts, blt, waits=stat_gate)

        if upto == "p3t":
            P.emit("sync", lambda e: None,
                   waits=_tl(x1T_toks) + P.fence(*[k[2:] for k in P.val if k.startswith("D_")]))
            AR.release()
            return _finish(P, block, nc)
        hw_toks = []
        last_hw = [None, None]
        tk_r = x1T_toks[-1]
        tk_x1ct = None
        for i in range(NTT):
            t_xc1 = P.dma("sync", "x1ct", x1ct[:, :, :],
                          x1t_s[:, i * 128:(i + 1) * 128].rearrange(
                              "(k p) t -> p k t", p=128),
                          waits=P.fence("gw0", "gw1") + [tk_x1ct])
            mmr = None
            for k in range(KD):
                mmr = MM(bank(0, 8), x1ct[:, k, :],
                         wrts[:, k * 8:(k + 1) * 8], (k == 0), (k == KD - 1),
                         waits=[t_xc1, tk_r], sig="pe" if k == KD - 1 else None)
            t_m1 = P.emit("vector",
                          lambda e: e.reduce_max(m1c[:, 0:1], bank(0, 8),
                                                 axis=AX.X),
                          waits=[mmr], sig="ve")
            t_m1 = VTS(m1c[:, 0:1], m1c[:, 0:1], -1.0, None, ALU.mult)
            t_z = ACT(zrow[:, :], bank(0, 8), AF.Exp, bias=m1c[:, 0:1],
                      waits=[t_m1])
            t_q = VTS(qrow[:, :], zrow[:, :], 1.0, None, ALU.is_lt, waits=[t_z])
            t_q = VTT(qrow[:, :], qrow[:, :], zrow[:, :], ALU.mult)
            t_m2 = P.emit("vector",
                          lambda e: e.reduce_max(m2c[:, 0:1], qrow[:, :],
                                                 axis=AX.X),
                          waits=[t_q], sig="ve")
            t_d = VTS(denc[:, 0:1], m2c[:, 0:1], 1.0, None, ALU.add,
                      waits=[t_m2])
            t_d = P.emit("vector",
                         lambda e: e.reciprocal(denc[:, 0:1], denc[:, 0:1]),
                         sig="ve")
            t_mk = VTS(qrow[:, :], zrow[:, :], m2c[:, 0:1], None, ALU.is_ge,
                       waits=[t_m2])
            t_w2 = VTT(zrow[:, :], zrow[:, :], qrow[:, :], ALU.mult,
                       waits=[t_mk])
            t_cb = VTS(combt[:, :], zrow[:, :], denc[:, 0:1], None, ALU.mult,
                       waits=[t_d, t_w2])
            mml = None
            for k in range(KD):
                mml = MM(bank(1, 128), x1ct[:, k, :],
                         alts[:, k * ER:(k + 1) * ER], (k == 0), (k == KD - 1),
                         waits=[t_alt, t_cb if k == 0 else None],
                         sig="pe" if k == KD - 1 else None)
            tk_x1ct = mml
            hpar = i % 2
            t_hw = P.emit("vector",
                          lambda e, hpar=hpar: e.scalar_tensor_tensor(
                              r3(hwstg[hpar][:, :], 16), r3(bank(1, 128), 16),
                              2.0, bfree(combt[:, :], 8, 16),
                              ALU.mult, ALU.mult),
                          waits=[mml, last_hw[hpar]], sig="ve")
            o_ps = bwin(2, 128)
            t_t = TR(o_ps, hwstg[hpar][:, :], idb[:, :], waits=[t_hw])
            t_e = VCOPY(hwT[:, i * 128:(i + 1) * 128], o_ps, waits=[t_t])
            last_hw[hpar] = t_e
            tk_r = t_e
            hw_toks.append(t_e)

        if upto == "p4a":
            P.emit("sync", lambda e: None,
                   waits=_tl(hw_toks) + P.fence(*[k[2:] for k in P.val if k.startswith("D_")]))
            AR.release()
            return _finish(P, block, nc)
        last_wbb = [[hw_toks[-1]] + _tl(last_wob)] * 4
        last_ffs = [None, None]
        last_xb2 = [None, None]
        last_aos = [[hw_toks[-1]]] * 4
        bank_free = [[hw_toks[-1]]] * 8
        ffp_toks = []
        for ob in range(8):
            t_w = []
            for j in range(4):
                t_w.append(P.dma("gpsimd", "wobl", wbb[j][:, :, :], wbt[ob * 4 + j],
                                 waits=[last_wbb[j]]))
            mm_last = [None] * 4
            for ci in range(NCOL):
                cw = min(512, NM - ci * 512)
                cpar = ci % 2
                for k in range(32):
                    gp = k % 4
                    t_xc2 = P.dma("sync" if gp < 2 else "scalar", f"gc{gp}",
                                  gcol[gp][:, 0:cw],
                                  x1t_s[k * 128:(k + 1) * 128,
                                        ci * 512: ci * 512 + cw],
                                  waits=[last_gcol[gp]] +
                                  (P.fence("gw0", "gw1")
                                   if (ob == 0 and ci == 0 and k < 4) else []))
                    tkk = None
                    for ot in range(4):
                        bk = ot + 4 * cpar
                        tkk = MM(bank(bk, cw), wbb[ot][:, k, :],
                                 gcol[gp][:, 0:cw], (k == 0), False,
                                 waits=[t_xc2, t_w[ot],
                                        bank_free[bk] if k == 0 else None],
                                 sig="pe" if ot == 3 else None)
                    last_gcol[gp] = tkk
                for ot in range(4):
                    bk = ot + 4 * cpar
                    mm_last[ot] = MM(bank(bk, cw),
                                     blts[:, (ob * 4 + ot) * 128:
                                          (ob * 4 + ot + 1) * 128],
                                     hwT[:, ci * 512: ci * 512 + cw],
                                     False, True,
                                     waits=[t_blt, hw_toks[-1]], sig="pe")
                for ot in range(4):
                    bk = ot + 4 * cpar
                    oidx = ob * 4 + ot
                    t_e = P.emit("vector",
                                 lambda e, bk=bk, cw=cw, ot=ot, oidx=oidx:
                                 e.tensor_scalar(aos[:, ot * 512: ot * 512 + cw],
                                                 bank(bk, cw),
                                                 bbcs[:, oidx:oidx + 1],
                                                 None, ALU.add),
                                 waits=[mm_last[ot], last_aos[ot]], sig="ve")
                    nsub = cw // 128
                    t_tr = None
                    for sub in range(nsub):
                        t_tr = TR(bank(bk, 128, sub * 128),
                                  aos[:, ot * 512 + sub * 128:
                                      ot * 512 + sub * 128 + 128],
                                  idf[:, :], waits=[t_e],
                                  sig="pe" if sub == nsub - 1 else None)
                    last_aos[ot] = t_tr
                    dlast = None
                    for sub in range(nsub):
                        xpar = sub % 2
                        mt0 = ci * 512 + sub * 128
                        tt = mt0 // 128
                        t_xb = P.dma("scalar", f"xb{xpar}", xblk[xpar][:, :],
                                     x1p_s[mt0:mt0 + 128,
                                           oidx * 128:(oidx + 1) * 128],
                                     waits=[last_xb2[xpar]] +
                                     (P.fence("x1w0", "x1w1")
                                      if last_xb2[xpar] is None else []))
                        t_a1 = P.emit("vector",
                                      lambda e, xpar=xpar, tt=tt:
                                      e.tensor_scalar(
                                          xblk[xpar][:, :], xblk[xpar][:, :],
                                          rs1[:, tt:tt + 1], nmr1[:, tt:tt + 1],
                                          ALU.mult, ALU.add),
                                      waits=[t_xb], sig="ve")
                        t_a2 = VTT(ffstg[xpar][:, :], bank(bk, 128, sub * 128),
                                   xblk[xpar][:, :], ALU.add,
                                   waits=[t_tr, t_a1, last_ffs[xpar]])
                        last_xb2[xpar] = t_a2
                        dlast = P.dma("sync", f"x1w{xpar}",
                                      ffp_s[mt0:mt0 + 128,
                                            oidx * 128:(oidx + 1) * 128],
                                      ffstg[xpar][:, :], waits=[t_a2])
                        last_ffs[xpar] = dlast
                    bank_free[bk] = dlast
            for j in range(4):
                last_wbb[j] = mm_last[j]
            ffp_toks.append(dlast)

        if upto == "p4b":
            P.emit("sync", lambda e: None,
                   waits=_tl(ffp_toks) + P.fence(*[k[2:] for k in P.val if k.startswith("D_")]))
            AR.release()
            return _finish(P, block, nc)
        out_toks = []
        tk_o = ffp_toks
        for i in range(NTT):
            t_l = P.dma("sync", "x1r", x1pt[:, :], ffp_s[i * 128:(i + 1) * 128, :],
                        waits=(P.fence("x1w0", "x1w1") if i == 0 else []) +
                        _tl(tk_o))
            t_mu = VTS(stat_sc[:, :], x1pt[:, :], 1.0 / HID, 0.0, ALU.mult,
                       ALU.add, waits=[t_l] + (_tl(mm_last) if i == 0 else []),
                       accum=mu1[:, i:i + 1])
            t_s2 = ACT(stat_sc[:, :], x1pt[:, :], AF.Square, scale=1.0 / 64.0,
                       waits=[t_mu], accum=exc[:, 0:1])
            t_v = VTT(vtmp[:, 0:1], mu1[:, i:i + 1], mu1[:, i:i + 1], ALU.mult,
                      waits=[t_mu])
            t_v = VTT(vtmp[:, 0:1], exc[:, 0:1], vtmp[:, 0:1], ALU.subtract,
                      waits=[t_s2])
            t_v = VTS(vtmp[:, 0:1], vtmp[:, 0:1], EPS, None, ALU.add)
            t_v = ACT(vtmp[:, 0:1], vtmp[:, 0:1], AF.Ln, waits=[t_v])
            t_v = ACT(rs1[:, i:i + 1], vtmp[:, 0:1], AF.Exp, scale=-0.5)
            t_n = VTT(nmr1[:, i:i + 1], mu1[:, i:i + 1], rs1[:, i:i + 1],
                      ALU.mult, waits=[t_v])
            t_n = VTS(nmr1[:, i:i + 1], nmr1[:, i:i + 1], -1.0, None, ALU.mult)
            t_o = P.emit("vector",
                         lambda e, i=i: e.tensor_scalar(
                             outt[:, :], x1pt[:, :], rs1[:, i:i + 1],
                             nmr1[:, i:i + 1], ALU.mult, ALU.add),
                         waits=[t_n] + (out_toks[-1:] if out_toks else []),
                         sig="ve")
            tk_o = t_o
            out_toks.append(P.dma("sync", "outw", out[i * 128:(i + 1) * 128, :],
                                  outt[:, :], waits=[t_o]))
        AR.release()

        P.emit("sync", lambda e: None, waits=[out_toks[-1]])
        return _finish(P, block, nc)


def _prep_w_tiles(w):
    o, i = w.shape
    e, kk = o // 128, i // 128
    return np.ascontiguousarray(
        w.reshape(e, 128, kk, 128).transpose(0, 3, 2, 1))


def make_inputs(x, weights, n_main):
    f = np.float32
    NM = n_main * CHUNK
    NR = PAD + NM
    xpad = np.concatenate([np.zeros((PAD, HID), f), x], 0)
    wint = _prep_w_tiles(np.asarray(weights["in_proj_w"], f))
    wotm = _prep_w_tiles(np.asarray(weights["out_proj_w"], f))
    wbtm = _prep_w_tiles(np.asarray(weights["W_base"], f))
    cw = np.asarray(weights["conv_w"], f).reshape(NE_HBC, 128, KCV)
    convw = np.ascontiguousarray(cw.transpose(1, 0, 2).reshape(128, NE_HBC * KCV))
    convb = np.ascontiguousarray(
        np.asarray(weights["conv_b"], f).reshape(NE_HBC, 128).T)
    nwcm = np.ascontiguousarray(
        np.asarray(weights["norm_w"], f).reshape(64, 128).T)
    bbcm = np.ascontiguousarray(
        np.asarray(weights["b_base"], f).reshape(32, 128).T)
    dtbc = np.asarray(weights["dt_bias"], f).reshape(128, 1)
    alogc = np.asarray(weights["A_log"], f).reshape(128, 1)
    drowm = np.asarray(weights["D"], f).reshape(1, 128)
    wrtm = np.ascontiguousarray(
        np.asarray(weights["W_router"], f).T.reshape(KD, 128, 8).transpose(1, 0, 2))
    altm = np.ascontiguousarray(
        np.asarray(weights["A"], f).reshape(ER, HID).T.reshape(KD, 128, ER)
        .transpose(1, 0, 2))
    bltm = np.ascontiguousarray(
        np.asarray(weights["B"], f).transpose(0, 2, 1).reshape(ER, HID))
    identf = np.eye(128, dtype=f)
    identb = np.eye(128).astype(ml_dtypes.bfloat16)
    tri = np.where(np.arange(128)[:, None] > np.arange(128)[None, :], NEG, 0.0)
    trimask = np.ascontiguousarray(tri.astype(f))
    onesrow = np.ones((1, 128), f)
    ncores = x.shape[0] // NM
    in_maps = []
    for c in range(ncores):
        r0 = c * NM
        smaskv = np.full((128, 1), 0.0 if c == 0 else 1.0, f)
        in_maps.append(dict(
            xin=np.ascontiguousarray(xpad[r0:r0 + NR]),
            wint=wint, wot=wotm, wbt=wbtm, convw=convw, convb=convb,
            nwc=nwcm, bbc=bbcm, dtbc=dtbc, alogc=alogc, drow=drowm,
            wrt=wrtm, alt=altm, blt=bltm, identf=identf, identb=identb,
            trimask=trimask, onesrow=onesrow, smask=smaskv))
    return in_maps


def kernel(L0_attn_x, L0_attn_in_proj_w, L0_attn_conv_w, L0_attn_conv_b,
           L0_attn_dt_bias, L0_attn_A_log, L0_attn_D, L0_attn_norm_w,
           L0_attn_out_proj_w, L0_ffn_W_base, L0_ffn_b_base, L0_ffn_W_router,
           L0_ffn_A, L0_ffn_B):
    x = np.asarray(L0_attn_x, np.float32)[0]
    S = x.shape[0]
    n_main = S // NCORES // CHUNK
    weights = dict(in_proj_w=L0_attn_in_proj_w, conv_w=L0_attn_conv_w,
                   conv_b=L0_attn_conv_b, dt_bias=L0_attn_dt_bias,
                   A_log=L0_attn_A_log, D=L0_attn_D, norm_w=L0_attn_norm_w,
                   out_proj_w=L0_attn_out_proj_w, W_base=L0_ffn_W_base,
                   b_base=L0_ffn_b_base, W_router=L0_ffn_W_router,
                   A=L0_ffn_A, B=L0_ffn_B)
    in_maps = make_inputs(x, weights, n_main)
    nc = build_program(n_main=n_main)
    res = run_bass_kernel_spmd(nc, in_maps, core_ids=list(range(NCORES)))
    outp = np.concatenate([res.results[c]["out"] for c in range(NCORES)], 0)
    return outp[None, :, :].astype(np.float32)
